# version 07 (final)
# baseline (speedup 1.0000x reference)
import numpy as np
import ml_dtypes
from contextlib import ExitStack

import concourse.bass as bass
import concourse.mybir as mybir
from concourse.bass_utils import run_bass_kernel_spmd
from concourse.bass_core_rust import IndirectOffsetOnAxis

F32 = mybir.dt.float32
BF16 = mybir.dt.bfloat16
I32 = mybir.dt.int32
AF = mybir.ActivationFunctionType
ALU = mybir.AluOpType
AXX = mybir.AxisListType.X

NC = 8
T = 8192
TC = T // NC
D = 2048
NF = D // 128
H = 16
HD = 128
NKC = T // 128
E = 8
CAP = 1280
DUMP = E * CAP
TBL_ROWS = 10368
MODSL = 12288 // NC
SC = float(HD) ** -0.5
LN_EPS = 1e-5

STAGE = 4
DBG = False


class Ctx:
    pass


def build_program(stage=None, dbg=None):
    stage = STAGE if stage is None else stage
    dbg = DBG if dbg is None else dbg
    nc = bass.Bass("TRN2", target_bir_lowering=False)
    ctx = Ctx()
    ctx.stage = stage
    ctx.dbg = dbg
    es = ExitStack()

    ctx.x_in = nc.dram_tensor("x_in", [NF, 128, TC], F32, kind="ExternalInput").ap()
    ctx.wq_in = nc.dram_tensor("wq_in", [D, D], F32, kind="ExternalInput").ap()
    ctx.wo_in = nc.dram_tensor("wo_in", [D, D], F32, kind="ExternalInput").ap()
    ctx.w1_in = nc.dram_tensor("w1_in", [D, D], F32, kind="ExternalInput").ap()
    ctx.w2_in = nc.dram_tensor("w2_in", [D, D], F32, kind="ExternalInput").ap()
    ctx.wm_in = nc.dram_tensor("wm_in", [NF, 128, MODSL], F32, kind="ExternalInput").ap()
    ctx.c_in = nc.dram_tensor("c_in", [128, NF], F32, kind="ExternalInput").ap()
    ctx.bo_in = nc.dram_tensor("bo_in", [128, NF], F32, kind="ExternalInput").ap()
    ctx.gw_in = nc.dram_tensor("gw_in", [NF, 128, E], F32, kind="ExternalInput").ap()
    ctx.tri_my = nc.dram_tensor("tri_my", [64, E], F32, kind="ExternalInput").ap()
    ctx.tokids = nc.dram_tensor("tokids", [128, 8], F32, kind="ExternalInput").ap()
    ctx.cidcol = nc.dram_tensor("cidcol", [128, 1], F32, kind="ExternalInput").ap()
    ctx.tri128 = nc.dram_tensor("tri128", [128, 128], F32, kind="ExternalInput").ap()
    ctx.colones = nc.dram_tensor("colones", [128, 8, 8], F32, kind="ExternalInput").ap()
    ctx.identb = nc.dram_tensor("identb", [128, 128], F32, kind="ExternalInput").ap()
    ctx.identf = nc.dram_tensor("identf", [128, 128], F32, kind="ExternalInput").ap()
    ctx.onecol = nc.dram_tensor("onecol", [128, 1], F32, kind="ExternalInput").ap()

    ctx.out_d = nc.dram_tensor("out_d", [NF, 128, TC], F32, kind="ExternalOutput").ap()
    if dbg:
        ctx.dbg_qfm = nc.dram_tensor("dbg_qfm", [NF, 128, TC], F32, kind="ExternalOutput").ap()
        ctx.dbg_dk = nc.dram_tensor("dbg_dk", [NC, H, TC], F32, kind="ExternalOutput").ap()
        ctx.dbg_mod = nc.dram_tensor("dbg_mod", [NC, MODSL], F32, kind="ExternalOutput").ap()
        ctx.dbg_x1 = nc.dram_tensor("dbg_x1", [NF, 128, TC], F32, kind="ExternalOutput").ap()
        ctx.dbg_y = nc.dram_tensor("dbg_y", [NF, 128, TC], F32, kind="ExternalOutput").ap()
        ctx.dbg_d = nc.dram_tensor("dbg_d", [H, TC], F32, kind="ExternalOutput").ap()
        ctx.dbg_tbl = nc.dram_tensor("dbg_tbl", [TBL_ROWS, 2], F32, kind="ExternalOutput").ap()
        ctx.dbg_h2 = nc.dram_tensor("dbg_h2", [TC, D], F32, kind="ExternalOutput").ap()
        ctx.dbg_ho = nc.dram_tensor("dbg_ho", [CAP, D], F32, kind="ExternalOutput").ap()

    dt_ = nc.dram_tensor
    ctx.qfm_lh = [dt_("qfm_lA", [8, 128, TC], BF16), dt_("qfm_lB", [8, 128, TC], BF16)]
    ctx.qfm_gh = [dt_("qfm_gA", [NC, 8, 128, TC], BF16), dt_("qfm_gB", [NC, 8, 128, TC], BF16)]
    ctx.qtm_lh = [dt_("qtm_lA", [TC, D // 2], BF16), dt_("qtm_lB", [TC, D // 2], BF16)]
    ctx.qtm_gh = [dt_("qtm_gA", [NC, TC, D // 2], BF16), dt_("qtm_gB", [NC, TC, D // 2], BF16)]
    ctx.dk_l = dt_("dk_l", [H, TC], F32)
    ctx.dk_g = dt_("dk_g", [NC, H, TC], F32)
    ctx.mod_l = dt_("mod_l", [1, MODSL], F32)
    ctx.mod_g = dt_("mod_g", [NC, MODSL], F32)
    ctx.cnt_l = dt_("cnt_l", [E, E], F32)
    ctx.cnt_g = dt_("cnt_g", [64, E], F32)
    ctx.tbl_l = dt_("tbl_l", [TBL_ROWS, 2], F32)
    ctx.tbl_g = dt_("tbl_g", [TBL_ROWS, 2], F32)
    ctx.h2_l = dt_("h2_l", [TC, D], BF16)
    ctx.h2_g = dt_("h2_g", [NC, TC, D], BF16)
    ctx.ho_l = dt_("ho_l", [CAP, D], BF16)
    ctx.ho_g = dt_("ho_g", [NC, CAP, D], BF16)
    ctx.x1_d = dt_("x1_d", [NF, 128, TC], F32)
    ctx.brow = dt_("brow", [1, 2048], F32)
    ctx.brow8 = dt_("brow8", [E, E], F32)
    ctx.prow_d = dt_("prow_d", [10, 128], F32)

    def sb(name, shape, dtp):
        t = es.enter_context(nc.sbuf_tensor(name, shape, dtp))
        setattr(ctx, name, t)
        return t

    sb("xb", [128, NF, TC], BF16)
    sb("arena1", [128, 32768], BF16)
    sb("qh_fm", [128, NF, TC], BF16)
    sb("a3", [128, 16384], BF16)
    sb("xs", [128, 2, TC], F32)
    sb("eb", [128, 2, TC], BF16)
    sb("mnegb", [128, NC, H, 8], F32)
    sb("expmb", [128, NC, H, 8], BF16)
    sb("rows", [1, 2048], F32)
    sb("rows_bc", [128, 2, TC], F32)
    sb("tmpc", [128, 2, TC], F32)
    sb("silucb", [128, NF], F32)
    sb("c_t", [128, NF], F32)
    sb("bo_t", [128, NF], F32)
    sb("modc", [128, 6, NF], F32)
    sb("gacl", [128, NF], F32)
    sb("gbcl", [128, NF], F32)
    sb("gabo", [128, NF], F32)
    sb("onepca", [128, NF], F32)
    sb("onepcb", [128, NF], F32)
    sb("gw_t", [128, NF, E], F32)
    sb("trib", [128, 128], BF16)
    sb("colob", [128, 8, 8], BF16)
    sb("idb", [128, 128], BF16)
    sb("idf", [128, 128], F32)
    sb("modt", [96, 128], F32)
    sb("oneb", [128, 1], BF16)
    sb("trimyb", [64, E], BF16)
    sb("tokid_t", [128, 8], F32)
    sb("cid_t", [128, 1], F32)
    sb("pidx", [128, 1], F32)
    sb("iota8", [128, 8], F32)
    sb("lgT", [128, 8, E], F32)
    sb("el8", [128, 8, E], F32)
    sb("esum", [128, 8], F32)
    sb("m8", [128, 8], F32)
    sb("negm8", [128, 8], F32)
    sb("ohf", [128, 8, E], F32)
    sb("ohb", [128, 8, E], BF16)
    sb("etok", [128, 8], F32)
    sb("psel", [128, 8], F32)
    sb("keep", [128, 8], F32)
    sb("grank", [128, 8], F32)
    sb("incl", [128, 8, E], F32)
    sb("prefbc", [128, 8, E], F32)
    sb("cnt_sb", [E, E], F32)
    sb("cntf", [64, E], F32)
    sb("cntb", [64, E], BF16)
    sb("prefmy", [E, E], F32)
    sb("sidx_f", [128, 10], F32)
    sb("iidx", [128, 1], I32)
    sb("gidx", [128, 10], I32)
    sb("vals", [128, 2], F32)
    sb("zer2", [128, 81], F32)
    sb("tinfo", [128, 10, 2], F32)
    sb("pcol", [128, 10], F32)

    a1 = ctx.arena1
    ctx.wq0 = a1[:, 0:16384].rearrange("p (a n) -> p a n", a=NF)
    ctx.wq1 = a1[:, 16384:32768].rearrange("p (a n) -> p a n", a=NF)
    ctx.qtmst = a1[:, 16384:18432]
    ctx.h1b = ctx.a3[:, 0:16384].rearrange("p (a n) -> p a n", a=NF)
    ctx.wmodb = ctx.a3[:, 0:3072].rearrange("p (b n) -> p b n", b=2)
    ctx.kbuf = [a1[:, 0:8192].rearrange("p (r n) -> p r n", r=NC),
                a1[:, 16384:24576].rearrange("p (r n) -> p r n", r=NC)]
    ctx.vbuf = [a1[:, 8192:16384].rearrange("p (c n) -> p c n", c=NKC),
                a1[:, 24576:32768].rearrange("p (c n) -> p c n", c=NKC)]
    ctx.wo_half = [a1[:, 0:16384].rearrange("p (a m) -> p a m", a=NF),
                   a1[:, 16384:32768].rearrange("p (a m) -> p a m", a=NF)]
    ctx.y_fm = ctx.a3[:, 0:16384].rearrange("p (a n) -> p a n", a=NF)
    ctx.G_fm = a1[:, 0:20480].rearrange("p (a n) -> p a n", a=NF)
    ctx.w1s = a1[:, 20480:24576].rearrange("p (b a n) -> p b a n", b=2, a=NF)
    ctx.w2s = a1[:, 24576:28672].rearrange("p (b a n) -> p b a n", b=2, a=NF)
    ctx.gst = a1[:, 28672:32768].rearrange("p (b n) -> p b n", b=2)
    ctx.hout_fm = a1[:, 0:20480].rearrange("p (a n) -> p a n", a=NF)
    ctx.hmid_lo = ctx.qh_fm
    ctx.hmid_hi = ctx.a3[:, 0:8192].rearrange("p (a n) -> p a n", a=NF)
    ctx.stg = ctx.a3[:, 8192:12288].rearrange("p (b n) -> p b n", b=2)
    ctx.lgfm = ctx.tmpc[0:8, 0, :]
    ctx.t3 = ctx.tmpc[:, 1, :].rearrange("p (b n) -> p b n", b=2)
    ctx.p_bc = ctx.eb[:].rearrange("p b n -> p (b n)")[:, 0:CAP]
    ctx.xsq = ctx.eb
    ctx.wmodf = ctx.xs[:].rearrange("p b n -> p (b n)")[:, 0:MODSL]
    ctx.expmf = ctx.tmpc[:, 0, :].rearrange("p (r h c) -> p r h c", r=NC, h=H)

    ctx.psS = [es.enter_context(nc.psum_tensor(f"psS{i}", [128, 512], F32)) for i in range(3)]
    ctx.psC = es.enter_context(nc.psum_tensor("psC", [128, 1024], F32))
    ctx.psD = es.enter_context(nc.psum_tensor("psD", [128, 1024], F32))
    ctx.psT = es.enter_context(nc.psum_tensor("psT", [128, 1024], BF16))

    for name in [
        "s_cst", "g_cst", "g_xb", "g_wm", "g_wq", "t_modmm", "s_xsq", "t_statc", "s_rows1", "v_h1t", "s_h1", "t_qh", "v_qev", "t_tb", "v_tev", "s_qtmw", "s_qsq", "t_dk", "s_dkev", "s_dkw", "s_qfmw", "s_dkld", "s_kv", "v_vt", "t_sc", "sc_eb", "t_dav", "t_hd", "v_free", "s_dbc", "v_ynorm", "g_wo0", "g_wo1", "t_wo", "s_t3", "v_x1h", "s_x1w", "g_x1b", "s_rows2", "s_h2f", "v_h2b", "t_gate", "s_lg", "v_rt", "s_cntw", "s_prefw", "s_tblz", "v_sidx", "g_scat", "v_idx2", "g_tinfo", "v_gidx", "s_h2w", "g_Ggat", "t_gtb", "v_gtev", "g_w1", "g_w2", "t_w1", "sc_hm", "t_w2", "v_ho", "t_htb", "v_htev", "s_how", "v_moeidx", "g_moe", "v_keep", "t_mtb", "v_mtev", "s_xs2", "v_out", "s_out", "s_pr1", "s_pr2", "s_pr3", "t_h2b", "v_h2tev", "v_pidx", "g_cc", "s_one", "t_one", "v_one", "s_cev", "s_modx", "g_wm1", "s_kv1", "s_xs1", "g_w1b", "g_w2b", "v_fd", "v_rcp", "v_dkc", "s_x1w1", "g_pbc", "g_pr2", "s_out1", "g_qtm0", "g_qtm1",
    ]:
        setattr(ctx, name, es.enter_context(nc.semaphore(name)))

    ctx.nc = nc
    with es, nc.Block() as block:
        @block.sync
        def _(sync):
            emit_sync(ctx, sync)

        @block.gpsimd
        def _(g):
            emit_gpsimd(ctx, g)

        @block.tensor
        def _(t):
            emit_tensor(ctx, t)

        @block.scalar
        def _(s):
            emit_scalar(ctx, s)

        @block.vector
        def _(v):
            emit_vector(ctx, v)

    return nc, ctx


def emit_sync(ctx, sync):
    st = ctx.stage
    sync.dma_start(out=ctx.c_t[:], in_=ctx.c_in[:, :]).then_inc(ctx.s_cst, 16)
    sync.dma_start(out=ctx.bo_t[:], in_=ctx.bo_in[:, :]).then_inc(ctx.s_cst, 16)
    sync.dma_start(out=ctx.gw_t[:], in_=ctx.gw_in.rearrange("a p e -> p a e")).then_inc(ctx.s_cst, 16)
    sync.dma_start(out=ctx.idf[:], in_=ctx.identf[:, :]).then_inc(ctx.s_cst, 16)
    sync.dma_start(out=ctx.tokid_t[:], in_=ctx.tokids[:, :]).then_inc(ctx.s_cst, 16)
    sync.dma_start(out=ctx.cid_t[:], in_=ctx.cidcol[:, :]).then_inc(ctx.s_cst, 16)
    CST = 16 * 6

    for fi in range(NF):
        if fi >= 1:
            sync.wait_ge(ctx.t_modmm, fi)
        sync.dma_start(out=ctx.wmodf[:, :], in_=ctx.wm_in[fi, :, :]).then_inc(ctx.g_wm, 16)
    sync.wait_ge(ctx.v_one, 1)
    sync.dma_start(out=ctx.mod_l[:, :], in_=ctx.rows[0:1, 0:MODSL]).then_inc(ctx.s_modx, 32)
    sync.wait_ge(ctx.g_cc, 1)
    mv = ctx.mod_g.rearrange("r m -> (r m)").rearrange("(q p) -> q p", p=128)
    sync.dma_start(out=ctx.modt[:], in_=mv).then_inc(ctx.s_modx, 16)

    sync.wait_ge(ctx.v_one, 4)
    sync.dma_start(out=ctx.brow[:, :], in_=ctx.rows[0:1, :]).then_inc(ctx.s_pr1, 32)
    sync.wait_ge(ctx.s_pr1, 32)
    sync.dma_start(out=ctx.rows_bc[:],
                   in_=ctx.brow.rearrange("o (b n) -> o b n", b=2).broadcast_to((128, 2, 1024))
                   ).then_inc(ctx.s_rows1, 16)

    for fo in range(NF):
        if fo == 8:
            sync.wait_ge(ctx.v_qev, 16)
            sync.dma_start(out=ctx.qfm_lh[0].rearrange("a p n -> p a n"),
                           in_=ctx.qh_fm[:, 0:8, :]).then_inc(ctx.s_qfmw, 16)
        sync.wait_ge(ctx.s_dkev, fo + 1)
        sync.dma_start(out=ctx.dk_l[fo:fo + 1, :], in_=ctx.rows[0:1, 1024:2048]).then_inc(ctx.s_dkw, 16)
    sync.wait_ge(ctx.v_qev, 32)
    sync.dma_start(out=ctx.qfm_lh[1].rearrange("a p n -> p a n"),
                   in_=ctx.qh_fm[:, 8:16, :]).then_inc(ctx.s_qfmw, 16)
    if ctx.dbg:
        sync.wait_ge(ctx.g_cc, 4)
        sync.dma_start(out=ctx.dbg_dk[:, :, :], in_=ctx.dk_g[:, :, :]).then_inc(ctx.s_pr2, 16)
        sync.wait_ge(ctx.g_cc, 1)
        sync.dma_start(out=ctx.dbg_mod[:, :], in_=ctx.mod_g[:, :]).then_inc(ctx.s_pr2, 16)
    if st < 2:
        _finish(ctx, sync)
        return

    sync.wait_ge(ctx.g_cc, 4)
    sync.wait_ge(ctx.g_qtm0, 128)
    sync.dma_start(out=ctx.tmpc[:, 0, :].rearrange("p (k n) -> p k n", k=8),
                   in_=ctx.dk_g.rearrange("r h (c p) -> (r h c) p", p=128).rearrange(
                       "(k q) p -> q k p", q=128)).then_inc(ctx.s_dkld, 16)
    def _kv_load(h):
        if h >= 2:
            sync.wait_ge(ctx.t_dav, NKC * (h - 1))
        b = h % 2
        kvs = ctx.s_kv if b == 0 else ctx.s_kv1
        hh, hr = h // 8, h % 8
        sync.wait_ge(ctx.g_cc, 2 if hh == 0 else 5)
        sync.dma_start(out=ctx.kbuf[b][:, :, :],
                       in_=ctx.qfm_gh[hh][:, hr, :, :].rearrange("r p n -> p r n")).then_inc(kvs, 16)
        sync.wait_ge(ctx.g_cc, 3 if hh == 0 else 6)
        sync.dma_start(out=ctx.vbuf[b][:, :, :],
                       in_=ctx.qtm_gh[hh].rearrange("r (cl p) n -> p (r cl) n", p=128)[:, :, 128 * hr:128 * (hr + 1)]
                       ).then_inc(kvs, 16)

    def _dbounce(h):
        sync.wait_ge(ctx.v_rcp, h + 1)
        sync.dma_start(out=ctx.brow[:, 0:1024], in_=ctx.rows[0:1, 0:1024]).then_inc(ctx.s_pr1, 16)
        sync.wait_ge(ctx.s_pr1, 32 + 16 * (h + 1))
        if h >= 1:
            sync.wait_ge(ctx.v_ynorm, h)
        sync.dma_start(out=ctx.rows_bc[:, 0, :],
                       in_=ctx.brow[0:1, 0:1024].broadcast_to((128, 1024))).then_inc(ctx.s_dbc, 16)

    for h in range(H):
        _kv_load(h)
        if h >= 2:
            _dbounce(h - 2)
    _dbounce(H - 2)
    _dbounce(H - 1)
    for fo in range(NF):
        if fo >= 2:
            sync.wait_ge(ctx.v_x1h, 2 * (fo - 1))
            sync.wait_ge(ctx.s_x1w if fo % 2 == 0 else ctx.s_x1w1, 16 * (fo // 2))
        sync.dma_start(out=ctx.xs[:, fo % 2, :], in_=ctx.x_in[fo, :, :]).then_inc(
            ctx.s_xs2 if fo % 2 == 0 else ctx.s_xs1, 16)
        sync.wait_ge(ctx.v_x1h, 2 * (fo + 1))
        sync.dma_start(out=ctx.x1_d[fo, :, :], in_=ctx.xs[:, fo % 2, :]).then_inc(
            ctx.s_x1w if fo % 2 == 0 else ctx.s_x1w1, 16)
    if ctx.dbg:
        sync.wait_ge(ctx.s_x1w, 16 * NF // 2)
        sync.wait_ge(ctx.s_x1w1, 16 * NF // 2)
        sync.dma_start(out=ctx.dbg_x1[:, :, :], in_=ctx.x1_d[:, :, :]).then_inc(ctx.s_pr2, 16)
    if st < 3:
        _finish(ctx, sync)
        return

    sync.wait_ge(ctx.v_one, 7)
    sync.dma_start(out=ctx.brow[:, :], in_=ctx.rows[0:1, :]).then_inc(ctx.s_pr3, 32)
    sync.wait_ge(ctx.s_pr3, 32)
    sync.dma_start(out=ctx.rows_bc[:],
                   in_=ctx.brow.rearrange("o (b n) -> o b n", b=2).broadcast_to((128, 2, 1024))
                   ).then_inc(ctx.s_rows2, 16)
    sync.wait_ge(ctx.s_tblz, 1)
    tblv = ctx.tbl_l.rearrange("(p a) b -> p (a b)", p=128)
    sync.dma_start(out=tblv[:, 0:81], in_=ctx.zer2[:, :]).then_inc(ctx.s_tblz, 16)
    sync.dma_start(out=tblv[:, 81:162], in_=ctx.zer2[:, :]).then_inc(ctx.s_tblz, 16)
    for tt in range(8):
        sync.wait_ge(ctx.v_h2tev, 2 * (tt + 1))
        sync.dma_start(out=ctx.h2_l[128 * tt:128 * (tt + 1), :], in_=ctx.stg[:, tt % 2, :]).then_inc(ctx.s_h2w, 16)
    sync.wait_ge(ctx.s_cev, 1)
    sync.dma_start(out=ctx.cnt_l[:, :], in_=ctx.cnt_sb[:]).then_inc(ctx.s_cntw, 16)
    sync.wait_ge(ctx.g_cc, 7)
    sync.dma_start(out=ctx.cntf[:], in_=ctx.cnt_g[:, :]).then_inc(ctx.s_cntw, 16)
    sync.wait_ge(ctx.s_cev, 3)
    sync.dma_start(out=ctx.brow8[:, :], in_=ctx.prefmy[:]).then_inc(ctx.s_prefw, 16)
    sync.wait_ge(ctx.s_prefw, 16)
    sync.dma_start(out=ctx.prefbc[:],
                   in_=ctx.brow8.rearrange("c e -> (c e)")[None, :].broadcast_to((128, 64)).rearrange(
                       "p (c e) -> p c e", c=8)).then_inc(ctx.s_prefw, 16)
    if ctx.dbg:
        sync.wait_ge(ctx.g_cc, 8)
        sync.dma_start(out=ctx.dbg_tbl[:, :], in_=ctx.tbl_g[:, :]).then_inc(ctx.s_pr2, 16)
    if st < 4:
        _finish(ctx, sync)
        return

    sync.wait_ge(ctx.v_one, 9)
    sync.dma_start(out=ctx.prow_d.rearrange("j p -> p j"), in_=ctx.pcol[:], allow_slow_non_contiguous=True).then_inc(ctx.s_pr3, 16)

    for j in range(10):
        sync.wait_ge(ctx.v_htev, 2 * (j + 1))
        sync.dma_start(out=ctx.ho_l[128 * j:128 * (j + 1), :], in_=ctx.stg[:, j % 2, :]).then_inc(ctx.s_how, 16)
    for fo in range(NF):
        if fo >= 2:
            sync.wait_ge(ctx.v_out, fo - 1)
            sync.wait_ge(ctx.s_out if fo % 2 == 0 else ctx.s_out1, 16 * (fo // 2))
        sync.dma_start(out=ctx.xs[:, fo % 2, :], in_=ctx.x1_d[fo, :, :]).then_inc(
            ctx.s_xs2 if fo % 2 == 0 else ctx.s_xs1, 16)
        sync.wait_ge(ctx.v_out, fo + 1)
        sync.dma_start(out=ctx.out_d[fo, :, :], in_=ctx.xs[:, fo % 2, :]).then_inc(
            ctx.s_out if fo % 2 == 0 else ctx.s_out1, 16)
    _finish(ctx, sync)


def _finish(ctx, sync):
    st = ctx.stage
    if st >= 4:
        sync.wait_ge(ctx.s_out, 16 * NF // 2)
        sync.wait_ge(ctx.s_out1, 16 * NF // 2)
    elif st >= 2:
        sync.wait_ge(ctx.s_x1w, 16 * NF // 2)
        sync.wait_ge(ctx.s_x1w1, 16 * NF // 2)
    else:
        sync.wait_ge(ctx.s_qfmw, 16)
        sync.wait_ge(ctx.s_dkw, 16 * H)
        sync.wait_ge(ctx.g_cc, 6)
    if st == 3:
        sync.wait_ge(ctx.g_cc, 8)
        sync.wait_ge(ctx.g_cc, 9)
        sync.wait_ge(ctx.s_prefw, 32)
    if ctx.dbg:
        ns = {1: 2, 2: 3, 3: 4, 4: 4}[st]
        ng = {1: 1, 2: 2, 3: 3, 4: 4}[st]
        sync.wait_ge(ctx.s_pr2, 16 * ns)
        sync.wait_ge(ctx.g_pr2, 16 * ng)


def emit_gpsimd(ctx, g):
    st = ctx.stage
    g.iota(ctx.pidx[:], pattern=[[0, 1]], base=0, channel_multiplier=1,
           allow_small_or_imprecise_dtypes=True).then_inc(ctx.v_pidx, 1)
    g.iota(ctx.iota8[:], pattern=[[1, 8]], base=0, channel_multiplier=0,
           allow_small_or_imprecise_dtypes=True).then_inc(ctx.v_pidx, 1)
    g.memset(ctx.zer2[:], 0.0).then_inc(ctx.s_tblz, 1)
    g.dma_start(out=ctx.trib[:], in_=ctx.tri128[:, :]).then_inc(ctx.g_cst, 16)
    g.dma_start(out=ctx.colob[:], in_=ctx.colones[:, :, :]).then_inc(ctx.g_cst, 16)
    g.dma_start(out=ctx.idb[:], in_=ctx.identb[:, :]).then_inc(ctx.g_cst, 16)
    g.dma_start(out=ctx.trimyb[:], in_=ctx.tri_my[:, :]).then_inc(ctx.g_cst, 16)
    g.dma_start(out=ctx.oneb[:], in_=ctx.onecol[:, :]).then_inc(ctx.g_cst, 16)
    g.dma_start(out=ctx.xb[:], in_=ctx.x_in.rearrange("a p n -> p a n")).then_inc(ctx.g_xb, 16)

    for half in range(2):
        g.dma_start(out=(ctx.wq0 if half == 0 else ctx.wq1)[:, :, :],
                    in_=ctx.wq_in.rearrange("(a p) m -> p a m", p=128)[:, :, 1024 * half:1024 * (half + 1)]
                    ).then_inc(ctx.g_wq if half == 0 else ctx.g_wm1, 16)
    g.wait_ge(ctx.s_modx, 32)
    g.collective_compute("AllGather", ALU.bypass, replica_groups=[list(range(NC))],
                         ins=[ctx.mod_l.ap().opt()], outs=[ctx.mod_g.ap().opt()]).then_inc(ctx.g_cc, 1)
    qtm_pn = [ctx.qtm_lh[0].rearrange("(tt p) n -> p tt n", p=128),
              ctx.qtm_lh[1].rearrange("(tt p) n -> p tt n", p=128)]
    def _qtm_cast(fo):
        g.wait_ge(ctx.v_tev, fo + 1)
        g.dma_start(out=qtm_pn[fo // 8][:, :, 128 * (fo % 8):128 * (fo % 8) + 128],
                    in_=ctx.tmpc[:, fo % 2, :].rearrange("p (tt n) -> p tt n", tt=8)
                    ).then_inc(ctx.g_qtm0 if fo % 2 == 0 else ctx.g_qtm1, 16)
    for fo in range(8):
        _qtm_cast(fo)
    g.wait_ge(ctx.s_qfmw, 16)
    g.collective_compute("AllGather", ALU.bypass, replica_groups=[list(range(NC))],
                         ins=[ctx.qfm_lh[0].ap().opt()], outs=[ctx.qfm_gh[0].ap().opt()]).then_inc(ctx.g_cc, 1)
    g.wait_ge(ctx.g_qtm0, 64)
    g.wait_ge(ctx.g_qtm1, 64)
    g.collective_compute("AllGather", ALU.bypass, replica_groups=[list(range(NC))],
                         ins=[ctx.qtm_lh[0].ap().opt()], outs=[ctx.qtm_gh[0].ap().opt()]).then_inc(ctx.g_cc, 1)
    for fo in range(8, NF):
        _qtm_cast(fo)
    g.wait_ge(ctx.s_dkw, 16 * H)
    g.collective_compute("AllGather", ALU.bypass, replica_groups=[list(range(NC))],
                         ins=[ctx.dk_l.ap().opt()], outs=[ctx.dk_g.ap().opt()]).then_inc(ctx.g_cc, 1)
    g.wait_ge(ctx.s_qfmw, 32)
    g.collective_compute("AllGather", ALU.bypass, replica_groups=[list(range(NC))],
                         ins=[ctx.qfm_lh[1].ap().opt()], outs=[ctx.qfm_gh[1].ap().opt()]).then_inc(ctx.g_cc, 1)
    g.wait_ge(ctx.g_qtm0, 128)
    g.wait_ge(ctx.g_qtm1, 128)
    g.collective_compute("AllGather", ALU.bypass, replica_groups=[list(range(NC))],
                         ins=[ctx.qtm_lh[1].ap().opt()], outs=[ctx.qtm_gh[1].ap().opt()]).then_inc(ctx.g_cc, 1)
    if ctx.dbg:
        g.wait_ge(ctx.s_qfmw, 32)
        g.dma_start(out=ctx.dbg_qfm[0:8, :, :], in_=ctx.qfm_lh[0][:, :, :])
        g.dma_start(out=ctx.dbg_qfm[8:16, :, :], in_=ctx.qfm_lh[1][:, :, :]).then_inc(ctx.g_pr2, 16)
    if ctx.dbg and ctx.stage >= 2:
        g.wait_ge(ctx.v_ynorm, H)
        g.dma_start(out=ctx.dbg_y.rearrange("a p n -> p a n"), in_=ctx.y_fm[:, :, :]).then_inc(ctx.g_pr2, 16)
    if st < 2:
        return
    g.wait_ge(ctx.t_dav, NKC * 15)
    g.dma_start(out=ctx.wo_half[0][:, :, :],
                in_=ctx.wo_in.rearrange("(a p) m -> p a m", p=128)[:, :, 0:1024]).then_inc(ctx.g_wo0, 16)
    g.wait_ge(ctx.t_dav, NKC * 16)
    g.dma_start(out=ctx.wo_half[1][:, :, :],
                in_=ctx.wo_in.rearrange("(a p) m -> p a m", p=128)[:, :, 1024:2048]).then_inc(ctx.g_wo1, 16)
    if st < 3:
        return
    for fo in range(NF):
        g.wait_ge(ctx.s_x1w if fo % 2 == 0 else ctx.s_x1w1, 16 * (fo // 2 + 1))
        g.dma_start(out=ctx.xb[:, fo, :], in_=ctx.x1_d[fo, :, :]).then_inc(ctx.g_x1b, 16)
    g.wait_ge(ctx.s_cntw, 16)
    g.collective_compute("AllGather", ALU.bypass, replica_groups=[list(range(NC))],
                         ins=[ctx.cnt_l.ap().opt()], outs=[ctx.cnt_g.ap().opt()]).then_inc(ctx.g_cc, 1)
    g.wait_ge(ctx.s_tblz, 33)
    for cl in range(8):
        g.wait_ge(ctx.v_sidx, cl + 1)
        g.indirect_dma_start(
            out=ctx.tbl_l[:, :], out_offset=IndirectOffsetOnAxis(ctx.iidx[:], axis=0),
            in_=ctx.vals[:], in_offset=None).then_inc(ctx.g_scat, 16)
    g.wait_ge(ctx.g_scat, 16 * 8)
    g.collective_compute("AllReduce", ALU.add, replica_groups=[list(range(NC))],
                         ins=[ctx.tbl_l.ap().opt()], outs=[ctx.tbl_g.ap().opt()]).then_inc(ctx.g_cc, 1)
    g.wait_ge(ctx.s_h2w, 16 * 8)
    g.collective_compute("AllGather", ALU.bypass, replica_groups=[list(range(NC))],
                         ins=[ctx.h2_l.ap().opt()], outs=[ctx.h2_g.ap().opt()]).then_inc(ctx.g_cc, 1)
    if ctx.dbg:
        g.dma_start(out=ctx.dbg_h2[:, :], in_=ctx.h2_l[:, :]).then_inc(ctx.g_pr2, 16)
    if st < 4:
        return
    g.wait_ge(ctx.g_cc, 8)
    for j in range(10):
        g.wait_ge(ctx.v_idx2, j + 1)
        g.indirect_dma_start(
            out=ctx.tinfo[:, j, :], out_offset=None,
            in_=ctx.tbl_g[:, :], in_offset=IndirectOffsetOnAxis(ctx.iidx[:], axis=0)).then_inc(ctx.g_tinfo, 16)
    g.wait_ge(ctx.s_pr3, 48)
    g.dma_start(out=ctx.p_bc[:],
                in_=ctx.prow_d.rearrange("j p -> (j p)")[None, :].broadcast_to((128, CAP))
                ).then_inc(ctx.g_pbc, 16)
    g.wait_ge(ctx.g_cc, 9)
    hsrc = ctx.h2_g.rearrange("r t n -> (r t) n")
    g.wait_ge(ctx.v_gidx, 10)
    for j in range(10):
        if j >= 2:
            g.wait_ge(ctx.t_gtb, 2 * (j - 1))
        g.indirect_dma_start(
            out=ctx.gst[:, j % 2, :], out_offset=None,
            in_=hsrc, in_offset=IndirectOffsetOnAxis(ctx.gidx[:, j:j + 1], axis=0)).then_inc(ctx.g_Ggat, 16)
    for io in range(NF):
        if io >= 2:
            g.wait_ge(ctx.t_w1, 3 * (io - 1))
        g.dma_start(out=ctx.w1s[:, io % 2, :, :],
                    in_=ctx.w1_in.rearrange("(a p) m -> p a m", p=128)[:, :, 128 * io:128 * (io + 1)]
                    ).then_inc(ctx.g_w1 if io % 2 == 0 else ctx.g_w1b, 16)
    for fo in range(NF):
        if fo >= 2:
            g.wait_ge(ctx.t_w2, 3 * (fo - 1))
        g.dma_start(out=ctx.w2s[:, fo % 2, :, :],
                    in_=ctx.w2_in.rearrange("(a p) m -> p a m", p=128)[:, :, 128 * fo:128 * (fo + 1)]
                    ).then_inc(ctx.g_w2 if fo % 2 == 0 else ctx.g_w2b, 16)
    g.wait_ge(ctx.s_how, 16 * 10)
    if ctx.dbg:
        g.dma_start(out=ctx.dbg_ho[:, :], in_=ctx.ho_l[:, :]).then_inc(ctx.g_pr2, 16)
    g.collective_compute("AllGather", ALU.bypass, replica_groups=[list(range(NC))],
                         ins=[ctx.ho_l.ap().opt()], outs=[ctx.ho_g.ap().opt()]).then_inc(ctx.g_cc, 1)
    g.wait_ge(ctx.g_cc, 10)
    hosrc = ctx.ho_g.rearrange("r t n -> (r t) n")
    g.wait_ge(ctx.v_moeidx, 8)
    for cl in range(8):
        if cl >= 2:
            g.wait_ge(ctx.t_mtb, 2 * (cl - 1))
        g.indirect_dma_start(
            out=ctx.stg[:, cl % 2, :], out_offset=None,
            in_=hosrc, in_offset=IndirectOffsetOnAxis(ctx.gidx[:, cl:cl + 1], axis=0)).then_inc(ctx.g_moe, 16)


def emit_tensor(ctx, t):
    st = ctx.stage
    psS, psC, psD, psT = ctx.psS, ctx.psC, ctx.psD, ctx.psT
    t.wait_ge(ctx.s_one, 1)
    for fi in range(NF):
        t.wait_ge(ctx.g_wm, 16 * (fi + 1))
        for nt in range(3):
            ins = t.matmul(psS[nt][0:1, :], ctx.silucb[:, fi:fi + 1],
                           ctx.wmodf[:, 512 * nt:512 * (nt + 1)],
                           start=(fi == 0), stop=(fi == NF - 1))
        ins.then_inc(ctx.t_modmm, 1)
    t.wait_ge(ctx.s_modx, 48)
    t.wait_ge(ctx.v_one, 1)
    t.transpose(ctx.psS[0][0:128, 0:96], ctx.modt[:, :], ctx.idf[0:96, 0:96]).then_inc(ctx.t_modmm, 1)
    t.wait_ge(ctx.g_xb, 16)
    t.wait_ge(ctx.s_cst, 16 * 6)
    t.wait_ge(ctx.g_cst, 80)
    for fi in range(NF):
        t.wait_ge(ctx.s_xsq, fi + 1)
        for half in range(2):
            sl = slice(512 * half, 512 * (half + 1))
            t.matmul(psD[0:1, sl], ctx.oneb[:], ctx.xb[:, fi, sl],
                     start=(fi == 0), stop=(fi == NF - 1))
            ins = t.matmul(psC[0:1, sl], ctx.oneb[:], ctx.xsq[:, fi % 2, sl],
                           start=(fi == 0), stop=(fi == NF - 1))
        ins.then_inc(ctx.t_statc, 1)
    def _qtm_dk(fc):
        t.wait_ge(ctx.v_qev, 2 * fc + 2)
        if fc >= 1:
            t.wait_ge(ctx.v_tev, fc)
        for tt in range(8):
            ins = t.transpose(psT[:, 128 * tt:128 * (tt + 1)],
                              ctx.qh_fm[:, fc, 128 * tt:128 * (tt + 1)], ctx.idb[:])
        ins.then_inc(ctx.t_tb, 1)
        if fc == 0:
            t.wait_ge(ctx.v_one, 4)
        t.wait_ge(ctx.s_qsq, fc + 1)
        for half in range(2):
            sl = slice(512 * half, 512 * (half + 1))
            ins = t.matmul(psD[0:1, sl], ctx.oneb[:], ctx.xsq[:, fc % 2, sl],
                           start=True, stop=True)
        ins.then_inc(ctx.t_dk, 1)

    t.wait_ge(ctx.s_h1, NF)
    t.wait_ge(ctx.v_one, 2)
    t.wait_ge(ctx.g_wq, 16)
    for fo in range(NF):
        if fo == 8:
            t.wait_ge(ctx.g_wm1, 16)
        wsrc = ctx.wq0 if fo < 8 else ctx.wq1
        for half in range(2):
            gi = 2 * fo + half
            if gi >= 3:
                t.wait_ge(ctx.v_qev, gi - 2)
            for fi in range(NF):
                ins = t.matmul(psS[gi % 3][:, :],
                               wsrc[:, fi, 128 * (fo % 8):128 * (fo % 8) + 128],
                               ctx.h1b[:, fi, 512 * half:512 * (half + 1)],
                               start=(fi == 0), stop=(fi == NF - 1))
            ins.then_inc(ctx.t_qh, 1)
        if fo >= 2:
            _qtm_dk(fo - 2)
    _qtm_dk(14)
    _qtm_dk(15)
    if st < 2:
        return
    t.wait_ge(ctx.s_dkld, 16)
    for r in range(NC):
        ins = t.transpose(psC[0:128, 128 * r:128 * (r + 1)],
                          ctx.tmpc[:, 0, 128 * r:128 * (r + 1)], ctx.idf[:])
    ins.then_inc(ctx.t_tb, 1)
    t.wait_ge(ctx.s_one, 3)
    for h in range(H):
        b = h % 2
        for c in range(NKC):
            it = h * NKC + c
            for half in range(2):
                j = 2 * it + half
                if j >= 3:
                    t.wait_ge(ctx.sc_eb, j - 2)
                if c == 0 and half == 0:
                    t.wait_ge(ctx.s_kv if h % 2 == 0 else ctx.s_kv1, 32 * (h // 2 + 1))
                t.matmul(psS[j % 3][:, :],
                         ctx.kbuf[b][:, c // 8, 128 * (c % 8):128 * (c % 8) + 128],
                         ctx.qh_fm[:, h, 512 * half:512 * (half + 1)],
                         start=True, stop=True).then_inc(ctx.t_sc, 1)
            if c >= 1:
                _dav(ctx, t, h, c - 1)
        _dav(ctx, t, h, NKC - 1)
    t.wait_ge(ctx.v_ynorm, H)
    for fo in range(NF):
        t.wait_ge(ctx.g_wo0 if fo < 8 else ctx.g_wo1, 16)
        wsrc = ctx.wo_half[0] if fo < 8 else ctx.wo_half[1]
        for half in range(2):
            gi = 2 * fo + half
            if gi >= 3:
                t.wait_ge(ctx.s_t3, gi - 2)
            for fi in range(NF):
                ins = t.matmul(psS[gi % 3][:, :],
                               wsrc[:, fi, 128 * (fo % 8):128 * (fo % 8) + 128],
                               ctx.y_fm[:, fi, 512 * half:512 * (half + 1)],
                               start=(fi == 0), stop=(fi == NF - 1))
            ins.then_inc(ctx.t_wo, 1)
    if st < 3:
        return
    t.wait_ge(ctx.g_x1b, 16 * NF)
    for fi in range(NF):
        t.wait_ge(ctx.s_xsq, NF + fi + 1)
        for half in range(2):
            sl = slice(512 * half, 512 * (half + 1))
            t.matmul(psD[0:1, sl], ctx.oneb[:], ctx.xb[:, fi, sl],
                     start=(fi == 0), stop=(fi == NF - 1))
            ins = t.matmul(psC[0:1, sl], ctx.oneb[:], ctx.xsq[:, fi % 2, sl],
                           start=(fi == 0), stop=(fi == NF - 1))
        ins.then_inc(ctx.t_statc, 1)
    t.wait_ge(ctx.v_one, 7)
    for ch in range(NF):
        t.wait_ge(ctx.s_h2f, ch + 1)
        for half in range(2):
            ins = t.matmul(psD[0:8, 512 * half:512 * (half + 1)],
                           ctx.gw_t[:, ch, :],
                           ctx.tmpc[:, ch % 2, 512 * half:512 * (half + 1)],
                           start=(ch == 0), stop=(ch == NF - 1))
        ins.then_inc(ctx.t_gate, 1)
    t.wait_ge(ctx.s_lg, 1)
    for cl in range(8):
        ins = t.transpose(psC[0:128, 8 * cl:8 * cl + 8], ctx.lgfm[0:8, 128 * cl:128 * (cl + 1)], ctx.idf[0:8, 0:8])
    ins.then_inc(ctx.t_one, 1)
    t.wait_ge(ctx.v_h2b, NF)
    t.wait_ge(ctx.v_tev, 16)
    for tt in range(8):
        for bb in range(2):
            if 2 * tt + bb >= 1:
                t.wait_ge(ctx.v_h2tev, 2 * tt + bb)
            for fo in range(8 * bb, 8 * bb + 8):
                ins = t.transpose(psT[:, 128 * (fo % 8):128 * (fo % 8) + 128],
                                  ctx.xb[:, fo, 128 * tt:128 * (tt + 1)], ctx.idb[:])
            ins.then_inc(ctx.t_h2b, 1)
    t.wait_ge(ctx.v_rt, 8)
    t.wait_ge(ctx.s_lg, 2)
    for cl in range(8):
        ins = t.matmul(psD[0:8, 0:8], ctx.colob[:, cl, :], ctx.ohb[:, cl, :],
                       start=(cl == 0), stop=(cl == 7))
    ins.then_inc(ctx.t_one, 1)
    for cl in range(8):
        ins = t.matmul(psC[0:128, 8 * cl:8 * cl + 8], ctx.trib[:], ctx.ohb[:, cl, :],
                       start=True, stop=True)
    ins.then_inc(ctx.t_one, 1)
    t.wait_ge(ctx.v_one, 8)
    t.wait_ge(ctx.s_cev, 1)
    t.matmul(psD[0:8, 8:16], ctx.trimyb[:, :], ctx.cntb[:, :], start=True, stop=True).then_inc(ctx.t_one, 1)
    if st < 4:
        return
    t.wait_ge(ctx.v_h2tev, 16)
    for j in range(10):
        t.wait_ge(ctx.g_Ggat, 32 * (j // 2 + 1))
        for bb in range(2):
            if 2 * j + bb >= 1:
                t.wait_ge(ctx.v_gtev, 2 * j + bb)
            for fi in range(8 * bb, 8 * bb + 8):
                ins = t.transpose(psT[:, 128 * (fi % 8):128 * (fi % 8) + 128],
                                  ctx.gst[:, j % 2, 128 * fi:128 * (fi + 1)], ctx.idb[:])
            ins.then_inc(ctx.t_gtb, 1)
    t.wait_ge(ctx.v_gtev, 20)
    for io in range(NF):
        t.wait_ge(ctx.g_w1 if io % 2 == 0 else ctx.g_w1b, 16 * (io // 2 + 1))
        for st3 in range(3):
            gi = 3 * io + st3
            if gi >= 3:
                t.wait_ge(ctx.sc_hm, gi - 2)
            n0 = 512 * st3
            n1 = min(CAP, n0 + 512)
            for fi in range(NF):
                ins = t.matmul(psS[gi % 3][:, 0:n1 - n0],
                               ctx.w1s[:, io % 2, fi, :],
                               ctx.G_fm[:, fi, n0:n1],
                               start=(fi == 0), stop=(fi == NF - 1))
            ins.then_inc(ctx.t_w1, 1)
    t.wait_ge(ctx.sc_hm, 48)
    for fo in range(NF):
        t.wait_ge(ctx.g_w2 if fo % 2 == 0 else ctx.g_w2b, 16 * (fo // 2 + 1))
        for st3 in range(3):
            gi = 3 * fo + st3
            if gi >= 3:
                t.wait_ge(ctx.v_ho, gi - 2)
            n0 = 512 * st3
            n1 = min(CAP, n0 + 512)
            for io in range(NF):
                hm = ctx.hmid_lo[:, io, n0:n1] if n1 <= 1024 else ctx.hmid_hi[:, io, n0 - 1024:n1 - 1024]
                ins = t.matmul(psS[gi % 3][:, 0:n1 - n0],
                               ctx.w2s[:, fo % 2, io, :], hm,
                               start=(io == 0), stop=(io == NF - 1))
            ins.then_inc(ctx.t_w2, 1)
    t.wait_ge(ctx.v_ho, 48)
    for j in range(10):
        for bb in range(2):
            if 2 * j + bb >= 1:
                t.wait_ge(ctx.v_htev, 2 * j + bb)
            for fo in range(8 * bb, 8 * bb + 8):
                ins = t.transpose(psT[:, 128 * (fo % 8):128 * (fo % 8) + 128],
                                  ctx.hout_fm[:, fo, 128 * j:128 * (j + 1)], ctx.idb[:])
            ins.then_inc(ctx.t_htb, 1)
    t.wait_ge(ctx.v_htev, 20)
    for cl in range(8):
        t.wait_ge(ctx.v_keep, cl + 1)
        for bb in range(2):
            if 2 * cl + bb >= 1:
                t.wait_ge(ctx.v_mtev, 8 * (2 * cl + bb))
            for fo in range(8 * bb, 8 * bb + 8):
                ins = t.transpose(psT[:, 128 * (fo % 8):128 * (fo % 8) + 128],
                                  ctx.stg[:, cl % 2, 128 * fo:128 * (fo + 1)], ctx.idb[:])
            ins.then_inc(ctx.t_mtb, 1)


def _dav(ctx, t, h, cp):
    it = h * NKC + cp
    t.wait_ge(ctx.sc_eb, 2 * it + 2)
    if cp == 0:
        if h == 0:
            t.wait_ge(ctx.v_dkc, 1)
        t.wait_ge(ctx.v_free, h)
        t.wait_ge(ctx.v_fd, h)
        t.wait_ge(ctx.v_vt, h + 1)
    b = h % 2
    ebv = ctx.eb[:, cp % 2, :]
    st_ = (cp == 0)
    sp = (cp == NKC - 1)
    t.matmul(ctx.psD[0:1, 0:512], ctx.expmb[:, cp // 8, h, cp % 8:cp % 8 + 1], ebv[:, 0:512], start=st_, stop=sp)
    t.matmul(ctx.psD[0:1, 512:1024], ctx.expmb[:, cp // 8, h, cp % 8:cp % 8 + 1], ebv[:, 512:1024], start=st_, stop=sp)
    t.matmul(ctx.psC[:, 0:512], ctx.vbuf[b][:, cp, :], ebv[:, 0:512], start=st_, stop=sp)
    t.matmul(ctx.psC[:, 512:1024], ctx.vbuf[b][:, cp, :], ebv[:, 512:1024], start=st_, stop=sp).then_inc(ctx.t_dav, 1)


def emit_scalar(ctx, s):
    st = ctx.stage
    psS, psC, psD = ctx.psS, ctx.psC, ctx.psD
    s.wait_ge(ctx.s_cst, 16 * 6)
    s.activation(ctx.silucb[:], ctx.c_t[:], AF.Silu).then_inc(ctx.s_one, 1)
    s.wait_ge(ctx.g_xb, 16)
    for fi in range(NF):
        if fi >= 2:
            s.wait_ge(ctx.t_statc, fi - 1)
        s.activation(ctx.xsq[:, fi % 2, :], ctx.xb[:, fi, :], AF.Square).then_inc(ctx.s_xsq, 1)
    s.wait_ge(ctx.v_one, 3)
    s.activation(ctx.tmpc[0:1, 1, 0:1024], ctx.rows[0:1, 0:1024], AF.Sqrt).then_inc(ctx.s_one, 1)
    s.wait_ge(ctx.s_pr1, 32)
    for h in range(H):
        s.wait_ge(ctx.v_qev, 2 * h + 2)
        if h >= 2:
            s.wait_ge(ctx.t_dk, h - 1)
        s.activation(ctx.xsq[:, h % 2, :], ctx.qh_fm[:, h, :], AF.Square).then_inc(ctx.s_qsq, 1)
        s.wait_ge(ctx.t_dk, h + 1)
        if h >= 1:
            s.wait_ge(ctx.s_dkw, 16 * h)
        s.copy(ctx.rows[0:1, 1024:2048], psD[0:1, :]).then_inc(ctx.s_dkev, 1)
    if st < 2:
        return
    s.wait_ge(ctx.v_one, 5)
    s.activation(ctx.expmf[:], ctx.mnegb[:], AF.Exp, scale=-1.0)
    s.drain()
    s.copy(ctx.expmb[:], ctx.expmf[:]).then_inc(ctx.s_one, 1)
    for h in range(H):
        for c in range(NKC):
            it = h * NKC + c
            for half in range(2):
                j = 2 * it + half
                s.wait_ge(ctx.t_sc, j + 1)
                if it >= 2 and half == 0:
                    s.wait_ge(ctx.t_dav, it - 1)
                s.activation(ctx.eb[:, it % 2, 512 * half:512 * (half + 1)],
                             psS[j % 3][:, :], AF.Exp,
                             bias=ctx.mnegb[:, c // 8, h, c % 8:c % 8 + 1], scale=SC).then_inc(ctx.sc_eb, 1)
    if st < 3:
        return
    s.wait_ge(ctx.v_one, 6)
    s.activation(ctx.tmpc[0:1, 1, 0:1024], ctx.rows[0:1, 0:1024], AF.Sqrt).then_inc(ctx.s_one, 1)
    s.wait_ge(ctx.t_gate, NF)
    s.copy(ctx.lgfm[:], psD[0:8, :]).then_inc(ctx.s_lg, 1)
    s.wait_ge(ctx.t_one, 1)
    s.copy(ctx.lgT[:].rearrange("p c e -> p (c e)"), psC[0:128, 0:64]).then_inc(ctx.s_lg, 1)
    for cl in range(8):
        s.wait_ge(ctx.v_rt, cl + 1)
        s.activation(ctx.el8[:, cl, :], ctx.lgT[:, cl, :], AF.Exp,
                     bias=ctx.negm8[:, cl:cl + 1],
                     accum_out=ctx.esum[:, cl:cl + 1]).then_inc(ctx.s_lg, 1)
    s.wait_ge(ctx.t_one, 2)
    s.copy(ctx.cnt_sb[:], psD[0:8, 0:8]).then_inc(ctx.s_cev, 1)
    s.wait_ge(ctx.t_one, 3)
    s.copy(ctx.incl[:].rearrange("p c e -> p (c e)"), psC[0:128, 0:64]).then_inc(ctx.s_cev, 1)
    s.wait_ge(ctx.t_one, 4)
    s.copy(ctx.prefmy[:], psD[0:8, 8:16]).then_inc(ctx.s_cev, 1)
    if st < 4:
        return
    for io in range(NF):
        for st3 in range(3):
            gi = 3 * io + st3
            s.wait_ge(ctx.t_w1, gi + 1)
            n0 = 512 * st3
            n1 = min(CAP, n0 + 512)
            hm = ctx.hmid_lo[:, io, n0:n1] if n1 <= 1024 else ctx.hmid_hi[:, io, n0 - 1024:n1 - 1024]
            s.activation(hm, psS[gi % 3][:, 0:n1 - n0], AF.Gelu).then_inc(ctx.sc_hm, 1)


def emit_vector(ctx, v):
    st = ctx.stage
    psS, psC, psD, psT = ctx.psS, ctx.psC, ctx.psD, ctx.psT
    v.wait_ge(ctx.t_modmm, NF)
    for nt in range(3):
        ins = v.tensor_copy(ctx.rows[0:1, 512 * nt:512 * (nt + 1)], psS[nt][0:1, :])
    ins.then_inc(ctx.v_one, 1)
    v.wait_ge(ctx.t_modmm, NF + 1)
    v.wait_ge(ctx.s_cst, 16 * 6)
    v.tensor_copy(ctx.modc[:].rearrange("p a b -> p (a b)"), ctx.psS[0][0:128, 0:96])
    v.drain()
    for (gi_, dst) in ((2, ctx.gacl), (5, ctx.gbcl)):
        a = ctx.tmpc[:, 0, 0:NF]
        b = ctx.tmpc[:, 0, NF:2 * NF]
        m = ctx.tmpc[:, 0, 2 * NF:3 * NF]
        v.tensor_scalar(a, ctx.modc[:, gi_, :], 0.25, None, ALU.max)
        v.tensor_scalar(b, ctx.modc[:, gi_, :], -0.25, None, ALU.min)
        v.tensor_scalar(m, ctx.modc[:, gi_, :], 0.0, None, ALU.is_ge)
        v.drain()
        v.tensor_sub(dst[:], a, b)
        v.drain()
        v.tensor_mul(dst[:], dst[:], m)
        v.drain()
        v.tensor_add(dst[:], dst[:], b)
        v.drain()
    v.tensor_scalar(ctx.onepca[:], ctx.modc[:, 1, :], 1.0, None, ALU.add)
    v.tensor_scalar(ctx.onepcb[:], ctx.modc[:, 4, :], 1.0, None, ALU.add)
    v.drain()
    v.tensor_mul(ctx.gabo[:], ctx.gacl[:], ctx.bo_t[:]).then_inc(ctx.v_one, 1)
    v.wait_ge(ctx.t_statc, 16)
    _ln_rows(ctx, v, 3, 2)
    v.wait_ge(ctx.s_rows1, 16)
    for ch in range(NF):
        v.tensor_tensor(ctx.tmpc[:, ch % 2, :], ctx.xb[:, ch, :], ctx.rows_bc[:, 0, :], ALU.mult)
        v.drain()
        v.tensor_tensor(ctx.tmpc[:, ch % 2, :], ctx.tmpc[:, ch % 2, :], ctx.rows_bc[:, 1, :], ALU.add)
        v.drain()
        v.tensor_scalar(ctx.h1b[:, ch, :], ctx.tmpc[:, ch % 2, :],
                        ctx.onepca[:, ch:ch + 1], ctx.modc[:, 0, ch:ch + 1],
                        ALU.mult, ALU.add).then_inc(ctx.s_h1, 1)
    def _qtm_evac(fc):
        if fc >= 2:
            v.wait_ge(ctx.g_qtm0 if fc % 2 == 0 else ctx.g_qtm1, 16 * (fc // 2))
        v.wait_ge(ctx.t_tb, fc + 1)
        v.tensor_copy(ctx.tmpc[:, fc % 2, :], psT[:, 0:1024]).then_inc(ctx.v_tev, 1)

    for gi in range(32):
        v.wait_ge(ctx.t_qh, gi + 1)
        fo, half = gi // 2, gi % 2
        v.tensor_copy(ctx.qh_fm[:, fo, 512 * half:512 * (half + 1)], psS[gi % 3][:, :]).then_inc(ctx.v_qev, 1)
        if half == 1 and fo >= 2:
            _qtm_evac(fo - 2)
    _qtm_evac(14)
    _qtm_evac(15)
    if st < 2:
        return
    v.wait_ge(ctx.t_tb, 17)
    v.tensor_copy(ctx.mnegb[:].rearrange("p r h c -> p (r h c)"), psC[0:128, :]).then_inc(ctx.v_dkc, 1)
    v.drain()
    v.tensor_scalar(ctx.mnegb[:], ctx.mnegb[:], -SC, None, ALU.mult).then_inc(ctx.v_one, 1)
    def _vhead_scale(h):
        b = h % 2
        v.wait_ge(ctx.s_kv if h % 2 == 0 else ctx.s_kv1, 32 * (h // 2 + 1))
        v.wait_ge(ctx.s_one, 3)
        if h >= 2:
            v.wait_ge(ctx.t_dav, NKC * (h - 1))
        for c in range(NKC):
            ins = v.tensor_scalar(ctx.vbuf[b][:, c, :], ctx.vbuf[b][:, c, :],
                                  ctx.expmf[:, c // 8, h, c % 8:c % 8 + 1], None, ALU.mult)
        ins.then_inc(ctx.v_vt, 1)

    def _vhead_end(h):
        v.wait_ge(ctx.t_dav, NKC * (h + 1))
        v.tensor_copy(ctx.y_fm[:, h, :], psC[:, :]).then_inc(ctx.v_free, 1)
        v.tensor_copy(ctx.rows[0:1, 0:1024], psD[0:1, :]).then_inc(ctx.v_fd, 1)
        v.drain()
        v.reciprocal(ctx.rows[0:1, 0:1024], ctx.rows[0:1, 0:1024]).then_inc(ctx.v_rcp, 1)
        v.wait_ge(ctx.s_dbc, 16 * (h + 1))
        v.drain()
        v.tensor_tensor(ctx.y_fm[:, h, :], ctx.y_fm[:, h, :], ctx.rows_bc[:, 0, :], ALU.mult).then_inc(ctx.v_ynorm, 1)

    for h in range(H):
        _vhead_scale(h)
        if h >= 1:
            _vhead_end(h - 1)
    _vhead_end(H - 1)
    v.wait_ge(ctx.g_qtm1, 128)
    for fo in range(NF):
        for half in range(2):
            gi = 2 * fo + half
            v.wait_ge(ctx.t_wo, gi + 1)
            v.wait_ge(ctx.s_xs2 if fo % 2 == 0 else ctx.s_xs1, 16 * (fo // 2 + 1))
            v.drain()
            v.tensor_scalar(ctx.t3[:, half, :], psS[gi % 3][:, :],
                            ctx.gacl[:, fo:fo + 1], ctx.gabo[:, fo:fo + 1],
                            ALU.mult, ALU.add).then_inc(ctx.s_t3, 1)
            v.drain()
            v.tensor_tensor(ctx.xs[:, fo % 2, 512 * half:512 * (half + 1)],
                            ctx.xs[:, fo % 2, 512 * half:512 * (half + 1)],
                            ctx.t3[:, half, :], ALU.add).then_inc(ctx.v_x1h, 1)
    if st < 3:
        return
    v.wait_ge(ctx.g_x1b, 16 * NF)
    for fi in range(NF):
        if fi >= 2:
            v.wait_ge(ctx.t_statc, NF + fi - 1)
        v.drain()
        v.tensor_tensor(ctx.xsq[:, fi % 2, :], ctx.xb[:, fi, :], ctx.xb[:, fi, :], ALU.mult).then_inc(ctx.s_xsq, 1)
    v.wait_ge(ctx.t_statc, 32)
    _ln_rows(ctx, v, 6, 4)
    v.wait_ge(ctx.s_rows2, 16)
    for ch in range(NF):
        if ch >= 2:
            v.wait_ge(ctx.t_gate, ch - 1)
        v.drain()
        v.tensor_tensor(ctx.tmpc[:, ch % 2, :], ctx.xb[:, ch, :], ctx.rows_bc[:, 0, :], ALU.mult)
        v.drain()
        v.tensor_tensor(ctx.tmpc[:, ch % 2, :], ctx.tmpc[:, ch % 2, :], ctx.rows_bc[:, 1, :], ALU.add)
        v.drain()
        v.tensor_scalar(ctx.tmpc[:, ch % 2, :], ctx.tmpc[:, ch % 2, :],
                        ctx.onepcb[:, ch:ch + 1], ctx.modc[:, 3, ch:ch + 1],
                        ALU.mult, ALU.add).then_inc(ctx.s_h2f, 1)
        v.drain()
        v.tensor_copy(ctx.xb[:, ch, :], ctx.tmpc[:, ch % 2, :]).then_inc(ctx.v_h2b, 1)
    for tt in range(8):
        if tt >= 1:
            v.wait_ge(ctx.s_h2w, 16 * tt)
        for bb in range(2):
            v.wait_ge(ctx.t_h2b, 2 * tt + bb + 1)
            v.tensor_copy(ctx.stg[:, tt % 2, 1024 * bb:1024 * (bb + 1)], psT[:, 0:1024]).then_inc(ctx.v_h2tev, 1)
    v.wait_ge(ctx.s_lg, 2)
    v.wait_ge(ctx.v_pidx, 2)
    for cl in range(8):
        _routing_chunk(ctx, v, cl)
    v.wait_ge(ctx.s_cntw, 32)
    v.tensor_copy(ctx.cntb[:], ctx.cntf[:]).then_inc(ctx.v_one, 1)
    v.wait_ge(ctx.s_cev, 2)
    v.wait_ge(ctx.s_prefw, 32)
    for cl in range(8):
        _grank_chunk(ctx, v, cl)
    if st < 4:
        return
    v.wait_ge(ctx.g_scat, 16 * 8)
    for j in range(10):
        if j >= 1:
            v.wait_ge(ctx.g_tinfo, 16 * j)
        v.drain()
        v.scalar_tensor_tensor(ctx.sidx_f[:, 0:1], ctx.cid_t[:], float(CAP), ctx.pidx[:], ALU.mult, ALU.add)
        v.drain()
        v.tensor_scalar(ctx.sidx_f[:, 0:1], ctx.sidx_f[:, 0:1], float(128 * j), None, ALU.add)
        v.drain()
        v.tensor_copy(ctx.iidx[:], ctx.sidx_f[:, 0:1]).then_inc(ctx.v_idx2, 1)
    v.wait_ge(ctx.g_tinfo, 160)
    v.drain()
    v.tensor_copy(ctx.pcol[:], ctx.tinfo[:, :, 1]).then_inc(ctx.v_one, 1)
    v.drain()
    v.tensor_copy(ctx.gidx[:], ctx.tinfo[:, :, 0]).then_inc(ctx.v_gidx, 10)
    for j in range(10):
        for bb in range(2):
            v.wait_ge(ctx.t_gtb, 2 * j + bb + 1)
            dst = ctx.G_fm[:, 8 * bb:8 * bb + 8, 128 * j:128 * (j + 1)]
            v.tensor_copy(dst, psT[:, 0:1024].rearrange("p (a n) -> p a n", a=8)).then_inc(ctx.v_gtev, 1)
    v.wait_ge(ctx.g_pbc, 16)
    for fo in range(NF):
        for st3 in range(3):
            gi = 3 * fo + st3
            v.wait_ge(ctx.t_w2, gi + 1)
            n0 = 512 * st3
            n1 = min(CAP, n0 + 512)
            v.tensor_tensor(ctx.hout_fm[:, fo, n0:n1], psS[gi % 3][:, 0:n1 - n0],
                            ctx.p_bc[:, n0:n1], ALU.mult).then_inc(ctx.v_ho, 1)
    for j in range(10):
        if j >= 1:
            v.wait_ge(ctx.s_how, 16 * j)
        for bb in range(2):
            v.wait_ge(ctx.t_htb, 2 * j + bb + 1)
            v.tensor_copy(ctx.stg[:, j % 2, 1024 * bb:1024 * (bb + 1)], psT[:, 0:1024]).then_inc(ctx.v_htev, 1)
    v.drain()
    v.scalar_tensor_tensor(ctx.sidx_f[:, 0:8], ctx.etok[:, :], float(CAP),
                           ctx.grank[:, :], ALU.mult, ALU.add)
    v.drain()
    v.tensor_mul(ctx.sidx_f[:, 0:8], ctx.sidx_f[:, 0:8], ctx.keep[:, :])
    v.drain()
    v.tensor_copy(ctx.gidx[:, 0:8], ctx.sidx_f[:, 0:8]).then_inc(ctx.v_moeidx, 8)
    for cl in range(8):
        v.wait_ge(ctx.g_moe, 32 * (cl // 2 + 1))
        v.drain()
        v.tensor_scalar(ctx.stg[:, cl % 2, :], ctx.stg[:, cl % 2, :],
                        ctx.keep[:, cl:cl + 1], None, ALU.mult).then_inc(ctx.v_keep, 1)
        for bb in range(2):
            v.wait_ge(ctx.t_mtb, 2 * cl + bb + 1)
            dst = ctx.xb[:, 8 * bb:8 * bb + 8, 128 * cl:128 * (cl + 1)]
            v.tensor_copy(dst, psT[:, 0:1024].rearrange("p (a n) -> p a n", a=8)).then_inc(ctx.v_mtev, 8)
    v.wait_ge(ctx.v_mtev, 128)
    for fo in range(NF):
        v.wait_ge(ctx.s_xs2 if fo % 2 == 0 else ctx.s_xs1, 16 * (NF // 2 + fo // 2 + 1))
        v.drain()
        v.scalar_tensor_tensor(ctx.xs[:, fo % 2, :], ctx.xb[:, fo, :],
                               ctx.gbcl[:, fo:fo + 1], ctx.xs[:, fo % 2, :],
                               ALU.mult, ALU.add).then_inc(ctx.v_out, 1)


def _ln_rows(ctx, v, v_val, s_val):
    r0 = ctx.rows[0:1, 0:1024]
    r1 = ctx.rows[0:1, 1024:2048]
    t0 = ctx.tmpc[0:1, 0, 0:1024]
    t1 = ctx.tmpc[0:1, 1, 0:1024]
    v.tensor_scalar(t0, ctx.psD[0:1, :], 1.0 / D, None, ALU.mult)
    v.tensor_scalar(t1, ctx.psC[0:1, :], 1.0 / D, None, ALU.mult)
    v.drain()
    v.tensor_tensor(r0, t0, t0, ALU.mult)
    v.drain()
    v.tensor_sub(r0, t1, r0)
    v.drain()
    v.tensor_scalar(r0, r0, LN_EPS, None, ALU.add).then_inc(ctx.v_one, 1)
    v.wait_ge(ctx.s_one, s_val)
    v.reciprocal(r1, t1)
    v.drain()
    v.tensor_tensor(t1, r1, r1, ALU.mult)
    v.drain()
    v.tensor_mul(t1, t1, r0)
    v.drain()
    v.tensor_scalar(t1, t1, -0.5, 1.5, ALU.mult, ALU.add)
    v.drain()
    v.tensor_mul(r1, r1, t1)
    v.drain()
    v.tensor_copy(r0, r1)
    v.drain()
    v.tensor_mul(r1, r1, t0)
    v.drain()
    v.tensor_scalar(r1, r1, -1.0, None, ALU.mult).then_inc(ctx.v_one, 1)


def _routing_chunk(ctx, v, cl):
    lg = ctx.lgT[:, cl, :]
    v.reduce_max(ctx.m8[:, cl:cl + 1], lg, AXX)
    v.drain()
    v.tensor_scalar(ctx.negm8[:, cl:cl + 1], ctx.m8[:, cl:cl + 1], -1.0, None, ALU.mult)
    v.drain()
    v.tensor_scalar(ctx.ohf[:, cl, :], lg, ctx.m8[:, cl:cl + 1], None, ALU.is_equal)
    v.drain()
    v.tensor_scalar(ctx.el8[:, cl, :], ctx.ohf[:, cl, :], -99.0, 99.0, ALU.mult, ALU.add)
    v.drain()
    v.tensor_add(ctx.el8[:, cl, :], ctx.el8[:, cl, :], ctx.iota8[:])
    v.drain()
    v.tensor_reduce(ctx.etok[:, cl:cl + 1], ctx.el8[:, cl, :], AXX, ALU.min)
    v.drain()
    v.tensor_scalar(ctx.ohf[:, cl, :], ctx.iota8[:], ctx.etok[:, cl:cl + 1], None, ALU.is_equal)
    v.drain()
    v.tensor_copy(ctx.ohb[:, cl, :], ctx.ohf[:, cl, :]).then_inc(ctx.v_rt, 1)


def _grank_chunk(ctx, v, cl):
    v.wait_ge(ctx.s_lg, 3 + cl)
    if cl >= 1:
        v.wait_ge(ctx.g_scat, 16 * cl)
    v.drain()
    v.tensor_tensor(ctx.el8[:, cl, :], ctx.el8[:, cl, :], ctx.ohf[:, cl, :], ALU.mult)
    v.drain()
    v.tensor_reduce(ctx.psel[:, cl:cl + 1], ctx.el8[:, cl, :], AXX, ALU.add)
    v.drain()
    v.reciprocal(ctx.esum[:, cl:cl + 1], ctx.esum[:, cl:cl + 1])
    v.drain()
    v.tensor_mul(ctx.psel[:, cl:cl + 1], ctx.psel[:, cl:cl + 1], ctx.esum[:, cl:cl + 1])
    v.drain()
    v.tensor_scalar(ctx.el8[:, cl, :], ctx.incl[:, cl, :], -1.0, None, ALU.add)
    v.drain()
    v.tensor_add(ctx.el8[:, cl, :], ctx.el8[:, cl, :], ctx.prefbc[:, cl, :])
    v.drain()
    v.tensor_mul(ctx.el8[:, cl, :], ctx.el8[:, cl, :], ctx.ohf[:, cl, :])
    v.drain()
    v.tensor_reduce(ctx.grank[:, cl:cl + 1], ctx.el8[:, cl, :], AXX, ALU.add)
    v.drain()
    v.tensor_scalar(ctx.keep[:, cl:cl + 1], ctx.grank[:, cl:cl + 1], float(CAP), None, ALU.is_lt)
    v.drain()
    v.scalar_tensor_tensor(ctx.sidx_f[:, cl:cl + 1], ctx.etok[:, cl:cl + 1], float(CAP),
                           ctx.grank[:, cl:cl + 1], ALU.mult, ALU.add)
    v.drain()
    v.tensor_scalar(ctx.sidx_f[:, cl:cl + 1], ctx.sidx_f[:, cl:cl + 1], float(-DUMP), None, ALU.add)
    v.drain()
    v.tensor_mul(ctx.sidx_f[:, cl:cl + 1], ctx.sidx_f[:, cl:cl + 1], ctx.keep[:, cl:cl + 1])
    v.drain()
    v.tensor_scalar(ctx.sidx_f[:, cl:cl + 1], ctx.sidx_f[:, cl:cl + 1], float(DUMP), None, ALU.add)
    v.drain()
    v.tensor_copy(ctx.iidx[:], ctx.sidx_f[:, cl:cl + 1])
    v.drain()
    v.tensor_copy(ctx.vals[:, 0:1], ctx.tokid_t[:, cl:cl + 1])
    v.drain()
    v.tensor_copy(ctx.vals[:, 1:2], ctx.psel[:, cl:cl + 1]).then_inc(ctx.v_sidx, 1)


def _prep_inputs(L0_attn_x, L0_attn_Wq, L0_attn_Wo, L0_attn_bo, L0_ffn_gate_weight,
                 L0_ffn_w1, L0_ffn_w2, L0_c, L0_wmod):
    bf = ml_dtypes.bfloat16
    x = np.asarray(L0_attn_x, np.float32).reshape(T, D)
    wq = np.ascontiguousarray(np.asarray(L0_attn_Wq, np.float32))
    wo = np.ascontiguousarray(np.asarray(L0_attn_Wo, np.float32))
    bo = np.asarray(L0_attn_bo, np.float32)
    gw = np.asarray(L0_ffn_gate_weight, np.float32)
    w1 = np.asarray(L0_ffn_w1, np.float32)
    w2 = np.asarray(L0_ffn_w2, np.float32)
    c = np.asarray(L0_c, np.float32).reshape(D)
    wm = np.asarray(L0_wmod, np.float32)

    tri128 = np.triu(np.ones((128, 128), np.float32))
    colones = np.zeros((128, 8, 8), np.float32)
    for cc in range(8):
        colones[:, cc, cc] = 1.0
    identb = np.eye(128, dtype=np.float32)
    identf = np.eye(128, dtype=np.float32)
    onecol = np.ones((128, 1), np.float32)

    in_maps = []
    for i in range(NC):
        sl = slice(TC * i, TC * (i + 1))
        x_i = np.ascontiguousarray(x[sl].T).reshape(NF, 128, TC)
        wm_i = np.ascontiguousarray(wm[:, MODSL * i:MODSL * (i + 1)]).reshape(NF, 128, MODSL)
        tri_my = np.zeros((64, E), np.float32)
        for clc in range(E):
            tri_my[0:8 * i + clc, clc] = 1.0
        tokids = np.ascontiguousarray(
            (TC * i + np.arange(TC, dtype=np.float32)).reshape(8, 128).T)
        in_maps.append({
            "x_in": x_i,
            "wq_in": wq, "wo_in": wo,
            "w1_in": np.ascontiguousarray(w1[i]),
            "w2_in": np.ascontiguousarray(w2[i]),
            "wm_in": wm_i,
            "c_in": np.ascontiguousarray(c.reshape(NF, 128).T),
            "bo_in": np.ascontiguousarray(bo.reshape(NF, 128).T),
            "gw_in": np.ascontiguousarray(gw.T.reshape(NF, 128, E)),
            "tri_my": tri_my,
            "tokids": tokids,
            "cidcol": np.full((128, 1), float(i), np.float32),
            "tri128": tri128, "colones": colones, "identb": identb,
            "identf": identf, "onecol": onecol,
        })
    return in_maps


def kernel(**inputs):
    in_maps = _prep_inputs(**inputs)
    nc, _ = build_program()
    res = run_bass_kernel_spmd(nc, in_maps, core_ids=list(range(NC)))
    outs = []
    for r in res.results:
        outs.append(r["out_d"].reshape(D, TC).T)
    return np.concatenate(outs, axis=0).reshape(1, T, D).astype(np.float32)


# version 08
# speedup vs baseline: 1.0438x; 1.0438x over previous
import numpy as np
import ml_dtypes
from contextlib import ExitStack

import concourse.bass as bass
import concourse.mybir as mybir
from concourse.bass_utils import run_bass_kernel_spmd
from concourse.bass_core_rust import IndirectOffsetOnAxis

F32 = mybir.dt.float32
BF16 = mybir.dt.bfloat16
I32 = mybir.dt.int32
AF = mybir.ActivationFunctionType
ALU = mybir.AluOpType
AXX = mybir.AxisListType.X

NC = 8
T = 8192
TC = T // NC
D = 2048
NF = D // 128
H = 16
HD = 128
NKC = T // 128
E = 8
CAP = 1280
DUMP = E * CAP
TBL_ROWS = 10368
MODSL = 12288 // NC
SC = float(HD) ** -0.5
LN_EPS = 1e-5

STAGE = 4
DBG = False


class Ctx:
    pass


def build_program(stage=None, dbg=None):
    stage = STAGE if stage is None else stage
    dbg = DBG if dbg is None else dbg
    nc = bass.Bass("TRN2", target_bir_lowering=False)
    ctx = Ctx()
    ctx.stage = stage
    ctx.dbg = dbg
    es = ExitStack()

    ctx.x_in = nc.dram_tensor("x_in", [NF, 128, TC], F32, kind="ExternalInput").ap()
    ctx.wq_in = nc.dram_tensor("wq_in", [D, D], F32, kind="ExternalInput").ap()
    ctx.wo_in = nc.dram_tensor("wo_in", [D, D], F32, kind="ExternalInput").ap()
    ctx.w1_in = nc.dram_tensor("w1_in", [D, D], F32, kind="ExternalInput").ap()
    ctx.w2_in = nc.dram_tensor("w2_in", [D, D], F32, kind="ExternalInput").ap()
    ctx.wm_in = nc.dram_tensor("wm_in", [NF, 128, MODSL], F32, kind="ExternalInput").ap()
    ctx.c_in = nc.dram_tensor("c_in", [128, NF], F32, kind="ExternalInput").ap()
    ctx.bo_in = nc.dram_tensor("bo_in", [128, NF], F32, kind="ExternalInput").ap()
    ctx.gw_in = nc.dram_tensor("gw_in", [NF, 128, E], F32, kind="ExternalInput").ap()
    ctx.tri_my = nc.dram_tensor("tri_my", [64, E], F32, kind="ExternalInput").ap()
    ctx.tokids = nc.dram_tensor("tokids", [128, 8], F32, kind="ExternalInput").ap()
    ctx.cidcol = nc.dram_tensor("cidcol", [128, 1], F32, kind="ExternalInput").ap()
    ctx.tri128 = nc.dram_tensor("tri128", [128, 128], F32, kind="ExternalInput").ap()
    ctx.colones = nc.dram_tensor("colones", [128, 8, 8], F32, kind="ExternalInput").ap()
    ctx.identb = nc.dram_tensor("identb", [128, 128], F32, kind="ExternalInput").ap()
    ctx.identf = nc.dram_tensor("identf", [128, 128], F32, kind="ExternalInput").ap()
    ctx.onecol = nc.dram_tensor("onecol", [128, 1], F32, kind="ExternalInput").ap()

    ctx.out_d = nc.dram_tensor("out_d", [NF, 128, TC], F32, kind="ExternalOutput").ap()
    if dbg:
        ctx.dbg_qfm = nc.dram_tensor("dbg_qfm", [NF, 128, TC], F32, kind="ExternalOutput").ap()
        ctx.dbg_dk = nc.dram_tensor("dbg_dk", [NC, H, TC], F32, kind="ExternalOutput").ap()
        ctx.dbg_mod = nc.dram_tensor("dbg_mod", [NC, MODSL], F32, kind="ExternalOutput").ap()
        ctx.dbg_x1 = nc.dram_tensor("dbg_x1", [NF, 128, TC], F32, kind="ExternalOutput").ap()
        ctx.dbg_y = nc.dram_tensor("dbg_y", [NF, 128, TC], F32, kind="ExternalOutput").ap()
        ctx.dbg_d = nc.dram_tensor("dbg_d", [H, TC], F32, kind="ExternalOutput").ap()
        ctx.dbg_tbl = nc.dram_tensor("dbg_tbl", [TBL_ROWS, 2], F32, kind="ExternalOutput").ap()
        ctx.dbg_h2 = nc.dram_tensor("dbg_h2", [TC, D], F32, kind="ExternalOutput").ap()
        ctx.dbg_ho = nc.dram_tensor("dbg_ho", [CAP, D], F32, kind="ExternalOutput").ap()

    dt_ = nc.dram_tensor
    ctx.qfm_lh = [dt_("qfm_lA", [8, 128, TC], BF16), dt_("qfm_lB", [8, 128, TC], BF16)]
    ctx.qfm_gh = [dt_("qfm_gA", [NC, 8, 128, TC], BF16), dt_("qfm_gB", [NC, 8, 128, TC], BF16)]
    ctx.qtm_lh = [dt_("qtm_lA", [TC, D // 2], BF16), dt_("qtm_lB", [TC, D // 2], BF16)]
    ctx.qtm_gh = [dt_("qtm_gA", [NC, TC, D // 2], BF16), dt_("qtm_gB", [NC, TC, D // 2], BF16)]
    ctx.dk_l = dt_("dk_l", [H, TC], F32)
    ctx.dk_g = dt_("dk_g", [NC, H, TC], F32)
    ctx.mod_l = dt_("mod_l", [1, MODSL], F32)
    ctx.mod_g = dt_("mod_g", [NC, MODSL], F32)
    ctx.cnt_l = dt_("cnt_l", [E, E], F32)
    ctx.cnt_g = dt_("cnt_g", [64, E], F32)
    ctx.tbl_l = dt_("tbl_l", [TBL_ROWS, 2], F32)
    ctx.tbl_g = dt_("tbl_g", [TBL_ROWS, 2], F32)
    ctx.h2_l = dt_("h2_l", [TC, D], BF16)
    ctx.h2_g = dt_("h2_g", [NC, TC, D], BF16)
    ctx.ho_l = dt_("ho_l", [CAP, D], BF16)
    ctx.ho_g = dt_("ho_g", [NC, CAP, D], BF16)
    ctx.x1_d = dt_("x1_d", [NF, 128, TC], F32)
    ctx.brow = dt_("brow", [1, 2048], F32)
    ctx.brow8 = dt_("brow8", [E, E], F32)
    ctx.prow_d = dt_("prow_d", [10, 128], F32)

    def sb(name, shape, dtp):
        t = es.enter_context(nc.sbuf_tensor(name, shape, dtp))
        setattr(ctx, name, t)
        return t

    sb("xb", [128, NF, TC], BF16)
    sb("arena1", [128, 32768], BF16)
    sb("qh_fm", [128, NF, TC], BF16)
    sb("a3", [128, 16384], BF16)
    sb("xs", [128, 2, TC], F32)
    sb("eb", [128, 2, TC], BF16)
    sb("mnegb", [128, NC, H, 8], F32)
    sb("expmb", [128, NC, H, 8], BF16)
    sb("rows", [1, 2048], F32)
    sb("rows_bc", [128, 2, TC], F32)
    sb("tmpc", [128, 2, TC], F32)
    sb("silucb", [128, NF], F32)
    sb("c_t", [128, NF], F32)
    sb("bo_t", [128, NF], F32)
    sb("modc", [128, 6, NF], F32)
    sb("gacl", [128, NF], F32)
    sb("gbcl", [128, NF], F32)
    sb("gabo", [128, NF], F32)
    sb("onepca", [128, NF], F32)
    sb("onepcb", [128, NF], F32)
    sb("gw_t", [128, NF, E], F32)
    sb("trib", [128, 128], BF16)
    sb("colob", [128, 8, 8], BF16)
    sb("idb", [128, 128], BF16)
    sb("idf", [128, 128], F32)
    sb("modt", [96, 128], F32)
    sb("oneb", [128, 1], BF16)
    sb("trimyb", [64, E], BF16)
    sb("tokid_t", [128, 8], F32)
    sb("cid_t", [128, 1], F32)
    sb("pidx", [128, 1], F32)
    sb("iota8", [128, 8], F32)
    sb("lgT", [128, 8, E], F32)
    sb("el8", [128, 8, E], F32)
    sb("esum", [128, 8], F32)
    sb("m8", [128, 8], F32)
    sb("negm8", [128, 8], F32)
    sb("ohf", [128, 8, E], F32)
    sb("ohb", [128, 8, E], BF16)
    sb("etok", [128, 8], F32)
    sb("psel", [128, 8], F32)
    sb("keep", [128, 8], F32)
    sb("grank", [128, 8], F32)
    sb("incl", [128, 8, E], F32)
    sb("prefbc", [128, 8, E], F32)
    sb("cnt_sb", [E, E], F32)
    sb("cntf", [64, E], F32)
    sb("cntb", [64, E], BF16)
    sb("prefmy", [E, E], F32)
    sb("sidx_f", [128, 10], F32)
    sb("iidx", [128, 1], I32)
    sb("gidx", [128, 10], I32)
    sb("vals", [128, 2], F32)
    sb("zer2", [128, 81], F32)
    sb("tinfo", [128, 10, 2], F32)
    sb("pcol", [128, 10], F32)

    a1 = ctx.arena1
    ctx.wq0 = a1[:, 0:16384].rearrange("p (a n) -> p a n", a=NF)
    ctx.wq1 = a1[:, 16384:32768].rearrange("p (a n) -> p a n", a=NF)
    ctx.qtmst = a1[:, 16384:18432]
    ctx.h1b = ctx.a3[:, 0:16384].rearrange("p (a n) -> p a n", a=NF)
    ctx.wmodb = ctx.a3[:, 0:3072].rearrange("p (b n) -> p b n", b=2)
    ctx.kbuf = [a1[:, 0:8192].rearrange("p (r n) -> p r n", r=NC),
                a1[:, 16384:24576].rearrange("p (r n) -> p r n", r=NC)]
    ctx.vbuf = [a1[:, 8192:16384].rearrange("p (c n) -> p c n", c=NKC),
                a1[:, 24576:32768].rearrange("p (c n) -> p c n", c=NKC)]
    ctx.wo_half = [a1[:, 0:16384].rearrange("p (a m) -> p a m", a=NF),
                   a1[:, 16384:32768].rearrange("p (a m) -> p a m", a=NF)]
    ctx.y_fm = ctx.a3[:, 0:16384].rearrange("p (a n) -> p a n", a=NF)
    ctx.G_fm = a1[:, 0:20480].rearrange("p (a n) -> p a n", a=NF)
    ctx.w1s = a1[:, 20480:24576].rearrange("p (b a n) -> p b a n", b=2, a=NF)
    ctx.w2s = a1[:, 24576:28672].rearrange("p (b a n) -> p b a n", b=2, a=NF)
    ctx.gst = a1[:, 28672:32768].rearrange("p (b n) -> p b n", b=2)
    ctx.hout_fm = a1[:, 0:20480].rearrange("p (a n) -> p a n", a=NF)
    ctx.hmid_lo = ctx.qh_fm
    ctx.hmid_hi = ctx.a3[:, 0:8192].rearrange("p (a n) -> p a n", a=NF)
    ctx.stg = ctx.a3[:, 8192:12288].rearrange("p (b n) -> p b n", b=2)
    ctx.lgfm = ctx.tmpc[0:8, 0, :]
    ctx.t3 = ctx.tmpc[:, 1, :].rearrange("p (b n) -> p b n", b=2)
    ctx.p_bc = ctx.eb[:].rearrange("p b n -> p (b n)")[:, 0:CAP]
    ctx.xsq = ctx.eb
    ctx.wmodf = ctx.xs[:].rearrange("p b n -> p (b n)")[:, 0:MODSL]
    ctx.expmf = ctx.tmpc[:, 0, :].rearrange("p (r h c) -> p r h c", r=NC, h=H)

    ctx.psS = [es.enter_context(nc.psum_tensor(f"psS{i}", [128, 512], F32)) for i in range(3)]
    ctx.psC = es.enter_context(nc.psum_tensor("psC", [128, 1024], F32))
    ctx.psD = es.enter_context(nc.psum_tensor("psD", [128, 1024], F32))
    ctx.psT = es.enter_context(nc.psum_tensor("psT", [128, 1024], BF16))

    for name in [
        "s_cst", "g_cst", "g_xb", "g_wm", "g_wq", "t_modmm", "s_xsq", "t_statc", "s_rows1", "v_h1t", "s_h1", "t_qh", "v_qev", "t_tb", "v_tev", "s_qtmw", "s_qsq", "t_dk", "s_dkev", "s_dkw", "s_qfmw", "s_dkld", "s_kv", "v_vt", "t_sc", "sc_eb", "t_dav", "t_hd", "v_free", "s_dbc", "v_ynorm", "g_wo0", "g_wo1", "t_wo", "s_t3", "v_x1h", "s_x1w", "g_x1b", "s_rows2", "s_h2f", "v_h2b", "t_gate", "s_lg", "v_rt", "s_cntw", "s_prefw", "s_tblz", "v_sidx", "g_scat", "v_idx2", "g_tinfo", "v_gidx", "s_h2w", "g_Ggat", "t_gtb", "v_gtev", "g_w1", "g_w2", "t_w1", "sc_hm", "t_w2", "v_ho", "t_htb", "v_htev", "s_how", "v_moeidx", "g_moe", "v_keep", "t_mtb", "v_mtev", "s_xs2", "v_out", "s_out", "s_pr1", "s_pr2", "s_pr3", "t_h2b", "v_h2tev", "v_pidx", "g_cc", "s_one", "t_one", "v_one", "s_cev", "s_modx", "g_wm1", "s_kv1", "s_xs1", "g_w1b", "g_w2b", "v_fd", "v_rcp", "v_dkc", "s_x1w1", "g_pbc", "g_pr2", "s_out1", "g_qtm0", "g_qtm1",
    ]:
        setattr(ctx, name, es.enter_context(nc.semaphore(name)))

    ctx.nc = nc
    with es, nc.Block() as block:
        @block.sync
        def _(sync):
            emit_sync(ctx, sync)

        @block.gpsimd
        def _(g):
            emit_gpsimd(ctx, g)

        @block.tensor
        def _(t):
            emit_tensor(ctx, t)

        @block.scalar
        def _(s):
            emit_scalar(ctx, s)

        @block.vector
        def _(v):
            emit_vector(ctx, v)

    return nc, ctx


def emit_sync(ctx, sync):
    st = ctx.stage
    sync.dma_start(out=ctx.c_t[:], in_=ctx.c_in[:, :]).then_inc(ctx.s_cst, 16)
    sync.dma_start(out=ctx.bo_t[:], in_=ctx.bo_in[:, :]).then_inc(ctx.s_cst, 16)
    sync.dma_start(out=ctx.gw_t[:], in_=ctx.gw_in.rearrange("a p e -> p a e")).then_inc(ctx.s_cst, 16)
    sync.dma_start(out=ctx.idf[:], in_=ctx.identf[:, :]).then_inc(ctx.s_cst, 16)
    sync.dma_start(out=ctx.tokid_t[:], in_=ctx.tokids[:, :]).then_inc(ctx.s_cst, 16)
    sync.dma_start(out=ctx.cid_t[:], in_=ctx.cidcol[:, :]).then_inc(ctx.s_cst, 16)
    CST = 16 * 6

    for fi in range(NF):
        if fi >= 1:
            sync.wait_ge(ctx.t_modmm, fi)
        sync.dma_start(out=ctx.wmodf[:, :], in_=ctx.wm_in[fi, :, :]).then_inc(ctx.g_wm, 16)
    sync.wait_ge(ctx.v_one, 1)
    sync.dma_start(out=ctx.mod_l[:, :], in_=ctx.rows[0:1, 0:MODSL]).then_inc(ctx.s_modx, 32)
    sync.wait_ge(ctx.g_cc, 1)
    mv = ctx.mod_g.rearrange("r m -> (r m)").rearrange("(q p) -> q p", p=128)
    sync.dma_start(out=ctx.modt[:], in_=mv).then_inc(ctx.s_modx, 16)

    sync.wait_ge(ctx.v_one, 4)
    sync.dma_start(out=ctx.brow[:, :], in_=ctx.rows[0:1, :]).then_inc(ctx.s_pr1, 32)
    sync.wait_ge(ctx.s_pr1, 32)
    sync.dma_start(out=ctx.rows_bc[:],
                   in_=ctx.brow.rearrange("o (b n) -> o b n", b=2).broadcast_to((128, 2, 1024))
                   ).then_inc(ctx.s_rows1, 16)

    for fo in range(NF):
        if fo == 8:
            sync.wait_ge(ctx.v_qev, 16)
            sync.dma_start(out=ctx.qfm_lh[0].rearrange("a p n -> p a n"),
                           in_=ctx.qh_fm[:, 0:8, :]).then_inc(ctx.s_qfmw, 16)
        sync.wait_ge(ctx.s_dkev, fo + 1)
        sync.dma_start(out=ctx.dk_l[fo:fo + 1, :], in_=ctx.rows[0:1, 1024:2048]).then_inc(ctx.s_dkw, 16)
    sync.wait_ge(ctx.v_qev, 32)
    sync.dma_start(out=ctx.qfm_lh[1].rearrange("a p n -> p a n"),
                   in_=ctx.qh_fm[:, 8:16, :]).then_inc(ctx.s_qfmw, 16)
    if ctx.dbg:
        sync.wait_ge(ctx.g_cc, 4)
        sync.dma_start(out=ctx.dbg_dk[:, :, :], in_=ctx.dk_g[:, :, :]).then_inc(ctx.s_pr2, 16)
        sync.wait_ge(ctx.g_cc, 1)
        sync.dma_start(out=ctx.dbg_mod[:, :], in_=ctx.mod_g[:, :]).then_inc(ctx.s_pr2, 16)
    if st < 2:
        _finish(ctx, sync)
        return

    sync.wait_ge(ctx.g_cc, 4)
    sync.wait_ge(ctx.g_qtm0, 128)
    sync.dma_start(out=ctx.tmpc[:, 0, :].rearrange("p (k n) -> p k n", k=8),
                   in_=ctx.dk_g.rearrange("r h (c p) -> (r h c) p", p=128).rearrange(
                       "(k q) p -> q k p", q=128)).then_inc(ctx.s_dkld, 16)
    def _kv_load(h):
        if h >= 2:
            sync.wait_ge(ctx.t_dav, NKC * (h - 1))
        b = h % 2
        kvs = ctx.s_kv if b == 0 else ctx.s_kv1
        hh, hr = h // 8, h % 8
        sync.wait_ge(ctx.g_cc, 2 if hh == 0 else 5)
        sync.dma_start(out=ctx.kbuf[b][:, :, :],
                       in_=ctx.qfm_gh[hh][:, hr, :, :].rearrange("r p n -> p r n")).then_inc(kvs, 16)
        sync.wait_ge(ctx.g_cc, 3 if hh == 0 else 6)
        sync.dma_start(out=ctx.vbuf[b][:, :, :],
                       in_=ctx.qtm_gh[hh].rearrange("r (cl p) n -> p (r cl) n", p=128)[:, :, 128 * hr:128 * (hr + 1)]
                       ).then_inc(kvs, 16)

    def _dbounce(h):
        sync.wait_ge(ctx.v_rcp, h + 1)
        sync.dma_start(out=ctx.brow[:, 0:1024], in_=ctx.rows[0:1, 0:1024]).then_inc(ctx.s_pr1, 16)
        sync.wait_ge(ctx.s_pr1, 32 + 16 * (h + 1))
        if h >= 1:
            sync.wait_ge(ctx.v_ynorm, h)
        sync.dma_start(out=ctx.rows_bc[:, 0, :],
                       in_=ctx.brow[0:1, 0:1024].broadcast_to((128, 1024))).then_inc(ctx.s_dbc, 16)

    for h in range(H):
        _kv_load(h)
        if h >= 2:
            _dbounce(h - 2)
    _dbounce(H - 2)
    _dbounce(H - 1)
    for fo in range(NF):
        if fo >= 2:
            sync.wait_ge(ctx.v_x1h, 2 * (fo - 1))
            sync.wait_ge(ctx.s_x1w if fo % 2 == 0 else ctx.s_x1w1, 16 * (fo // 2))
        sync.dma_start(out=ctx.xs[:, fo % 2, :], in_=ctx.x_in[fo, :, :]).then_inc(
            ctx.s_xs2 if fo % 2 == 0 else ctx.s_xs1, 16)
        sync.wait_ge(ctx.v_x1h, 2 * (fo + 1))
        sync.dma_start(out=ctx.x1_d[fo, :, :], in_=ctx.xs[:, fo % 2, :]).then_inc(
            ctx.s_x1w if fo % 2 == 0 else ctx.s_x1w1, 16)
    if ctx.dbg:
        sync.wait_ge(ctx.s_x1w, 16 * NF // 2)
        sync.wait_ge(ctx.s_x1w1, 16 * NF // 2)
        sync.dma_start(out=ctx.dbg_x1[:, :, :], in_=ctx.x1_d[:, :, :]).then_inc(ctx.s_pr2, 16)
    if st < 3:
        _finish(ctx, sync)
        return

    sync.wait_ge(ctx.v_one, 7)
    sync.dma_start(out=ctx.brow[:, :], in_=ctx.rows[0:1, :]).then_inc(ctx.s_pr3, 32)
    sync.wait_ge(ctx.s_pr3, 32)
    sync.dma_start(out=ctx.rows_bc[:],
                   in_=ctx.brow.rearrange("o (b n) -> o b n", b=2).broadcast_to((128, 2, 1024))
                   ).then_inc(ctx.s_rows2, 16)
    sync.wait_ge(ctx.s_tblz, 1)
    tblv = ctx.tbl_l.rearrange("(p a) b -> p (a b)", p=128)
    sync.dma_start(out=tblv[:, 0:81], in_=ctx.zer2[:, :]).then_inc(ctx.s_tblz, 16)
    sync.dma_start(out=tblv[:, 81:162], in_=ctx.zer2[:, :]).then_inc(ctx.s_tblz, 16)
    for tt in range(8):
        sync.wait_ge(ctx.v_h2tev, 2 * (tt + 1))
        sync.dma_start(out=ctx.h2_l[128 * tt:128 * (tt + 1), :], in_=ctx.stg[:, tt % 2, :]).then_inc(ctx.s_h2w, 16)
    sync.wait_ge(ctx.s_cev, 1)
    sync.dma_start(out=ctx.cnt_l[:, :], in_=ctx.cnt_sb[:]).then_inc(ctx.s_cntw, 16)
    sync.wait_ge(ctx.g_cc, 7)
    sync.dma_start(out=ctx.cntf[:], in_=ctx.cnt_g[:, :]).then_inc(ctx.s_cntw, 16)
    sync.wait_ge(ctx.s_cev, 3)
    sync.dma_start(out=ctx.brow8[:, :], in_=ctx.prefmy[:]).then_inc(ctx.s_prefw, 16)
    sync.wait_ge(ctx.s_prefw, 16)
    sync.dma_start(out=ctx.prefbc[:],
                   in_=ctx.brow8.rearrange("c e -> (c e)")[None, :].broadcast_to((128, 64)).rearrange(
                       "p (c e) -> p c e", c=8)).then_inc(ctx.s_prefw, 16)
    if ctx.dbg:
        sync.wait_ge(ctx.g_cc, 8)
        sync.dma_start(out=ctx.dbg_tbl[:, :], in_=ctx.tbl_g[:, :]).then_inc(ctx.s_pr2, 16)
    if st < 4:
        _finish(ctx, sync)
        return

    sync.wait_ge(ctx.v_one, 9)
    sync.dma_start(out=ctx.prow_d.rearrange("j p -> p j"), in_=ctx.pcol[:], allow_slow_non_contiguous=True).then_inc(ctx.s_pr3, 16)

    for j in range(10):
        sync.wait_ge(ctx.v_htev, 2 * (j + 1))
        sync.dma_start(out=ctx.ho_l[128 * j:128 * (j + 1), :], in_=ctx.stg[:, j % 2, :]).then_inc(ctx.s_how, 16)
    for fo in range(NF):
        if fo >= 2:
            sync.wait_ge(ctx.v_out, fo - 1)
            sync.wait_ge(ctx.s_out if fo % 2 == 0 else ctx.s_out1, 16 * (fo // 2))
        sync.dma_start(out=ctx.xs[:, fo % 2, :], in_=ctx.x1_d[fo, :, :]).then_inc(
            ctx.s_xs2 if fo % 2 == 0 else ctx.s_xs1, 16)
        sync.wait_ge(ctx.v_out, fo + 1)
        sync.dma_start(out=ctx.out_d[fo, :, :], in_=ctx.xs[:, fo % 2, :]).then_inc(
            ctx.s_out if fo % 2 == 0 else ctx.s_out1, 16)
    _finish(ctx, sync)


def _finish(ctx, sync):
    st = ctx.stage
    if st >= 4:
        sync.wait_ge(ctx.s_out, 16 * NF // 2)
        sync.wait_ge(ctx.s_out1, 16 * NF // 2)
    elif st >= 2:
        sync.wait_ge(ctx.s_x1w, 16 * NF // 2)
        sync.wait_ge(ctx.s_x1w1, 16 * NF // 2)
    else:
        sync.wait_ge(ctx.s_qfmw, 16)
        sync.wait_ge(ctx.s_dkw, 16 * H)
        sync.wait_ge(ctx.g_cc, 6)
    if st == 3:
        sync.wait_ge(ctx.g_cc, 8)
        sync.wait_ge(ctx.g_cc, 9)
        sync.wait_ge(ctx.s_prefw, 32)
    if ctx.dbg:
        ns = {1: 2, 2: 3, 3: 4, 4: 4}[st]
        ng = {1: 1, 2: 2, 3: 3, 4: 4}[st]
        sync.wait_ge(ctx.s_pr2, 16 * ns)
        sync.wait_ge(ctx.g_pr2, 16 * ng)


def emit_gpsimd(ctx, g):
    st = ctx.stage
    g.iota(ctx.pidx[:], pattern=[[0, 1]], base=0, channel_multiplier=1,
           allow_small_or_imprecise_dtypes=True).then_inc(ctx.v_pidx, 1)
    g.iota(ctx.iota8[:], pattern=[[1, 8]], base=0, channel_multiplier=0,
           allow_small_or_imprecise_dtypes=True).then_inc(ctx.v_pidx, 1)
    g.memset(ctx.zer2[:], 0.0).then_inc(ctx.s_tblz, 1)
    g.dma_start(out=ctx.trib[:], in_=ctx.tri128[:, :]).then_inc(ctx.g_cst, 16)
    g.dma_start(out=ctx.colob[:], in_=ctx.colones[:, :, :]).then_inc(ctx.g_cst, 16)
    g.dma_start(out=ctx.idb[:], in_=ctx.identb[:, :]).then_inc(ctx.g_cst, 16)
    g.dma_start(out=ctx.trimyb[:], in_=ctx.tri_my[:, :]).then_inc(ctx.g_cst, 16)
    g.dma_start(out=ctx.oneb[:], in_=ctx.onecol[:, :]).then_inc(ctx.g_cst, 16)
    for k in range(4):
        if k >= 1:
            g.wait_ge(ctx.g_xb, 16 * k)
        g.dma_start(out=ctx.xb[:, 4 * k:4 * (k + 1), :],
                    in_=ctx.x_in.rearrange("a p n -> p a n")[:, 4 * k:4 * (k + 1), :]
                    ).then_inc(ctx.g_xb, 16)

    for half in range(2):
        g.dma_start(out=(ctx.wq0 if half == 0 else ctx.wq1)[:, :, :],
                    in_=ctx.wq_in.rearrange("(a p) m -> p a m", p=128)[:, :, 1024 * half:1024 * (half + 1)]
                    ).then_inc(ctx.g_wq if half == 0 else ctx.g_wm1, 16)
    g.wait_ge(ctx.s_modx, 32)
    g.collective_compute("AllGather", ALU.bypass, replica_groups=[list(range(NC))],
                         ins=[ctx.mod_l.ap().opt()], outs=[ctx.mod_g.ap().opt()]).then_inc(ctx.g_cc, 1)
    qtm_pn = [ctx.qtm_lh[0].rearrange("(tt p) n -> p tt n", p=128),
              ctx.qtm_lh[1].rearrange("(tt p) n -> p tt n", p=128)]
    def _qtm_cast(fo):
        g.wait_ge(ctx.v_tev, fo + 1)
        g.dma_start(out=qtm_pn[fo // 8][:, :, 128 * (fo % 8):128 * (fo % 8) + 128],
                    in_=ctx.tmpc[:, fo % 2, :].rearrange("p (tt n) -> p tt n", tt=8)
                    ).then_inc(ctx.g_qtm0 if fo % 2 == 0 else ctx.g_qtm1, 16)
    for fo in range(8):
        _qtm_cast(fo)
    g.wait_ge(ctx.s_qfmw, 16)
    g.collective_compute("AllGather", ALU.bypass, replica_groups=[list(range(NC))],
                         ins=[ctx.qfm_lh[0].ap().opt()], outs=[ctx.qfm_gh[0].ap().opt()]).then_inc(ctx.g_cc, 1)
    g.wait_ge(ctx.g_qtm0, 64)
    g.wait_ge(ctx.g_qtm1, 64)
    g.collective_compute("AllGather", ALU.bypass, replica_groups=[list(range(NC))],
                         ins=[ctx.qtm_lh[0].ap().opt()], outs=[ctx.qtm_gh[0].ap().opt()]).then_inc(ctx.g_cc, 1)
    for fo in range(8, NF):
        _qtm_cast(fo)
    g.wait_ge(ctx.s_dkw, 16 * H)
    g.collective_compute("AllGather", ALU.bypass, replica_groups=[list(range(NC))],
                         ins=[ctx.dk_l.ap().opt()], outs=[ctx.dk_g.ap().opt()]).then_inc(ctx.g_cc, 1)
    g.wait_ge(ctx.s_qfmw, 32)
    g.collective_compute("AllGather", ALU.bypass, replica_groups=[list(range(NC))],
                         ins=[ctx.qfm_lh[1].ap().opt()], outs=[ctx.qfm_gh[1].ap().opt()]).then_inc(ctx.g_cc, 1)
    g.wait_ge(ctx.g_qtm0, 128)
    g.wait_ge(ctx.g_qtm1, 128)
    g.collective_compute("AllGather", ALU.bypass, replica_groups=[list(range(NC))],
                         ins=[ctx.qtm_lh[1].ap().opt()], outs=[ctx.qtm_gh[1].ap().opt()]).then_inc(ctx.g_cc, 1)
    if ctx.dbg:
        g.wait_ge(ctx.s_qfmw, 32)
        g.dma_start(out=ctx.dbg_qfm[0:8, :, :], in_=ctx.qfm_lh[0][:, :, :])
        g.dma_start(out=ctx.dbg_qfm[8:16, :, :], in_=ctx.qfm_lh[1][:, :, :]).then_inc(ctx.g_pr2, 16)
    if ctx.dbg and ctx.stage >= 2:
        g.wait_ge(ctx.v_ynorm, H)
        g.dma_start(out=ctx.dbg_y.rearrange("a p n -> p a n"), in_=ctx.y_fm[:, :, :]).then_inc(ctx.g_pr2, 16)
    if st < 2:
        return
    g.wait_ge(ctx.t_dav, NKC * 15)
    g.dma_start(out=ctx.wo_half[0][:, :, :],
                in_=ctx.wo_in.rearrange("(a p) m -> p a m", p=128)[:, :, 0:1024]).then_inc(ctx.g_wo0, 16)
    g.wait_ge(ctx.t_dav, NKC * 16)
    g.dma_start(out=ctx.wo_half[1][:, :, :],
                in_=ctx.wo_in.rearrange("(a p) m -> p a m", p=128)[:, :, 1024:2048]).then_inc(ctx.g_wo1, 16)
    if st < 3:
        return
    for fo in range(NF):
        g.wait_ge(ctx.s_x1w if fo % 2 == 0 else ctx.s_x1w1, 16 * (fo // 2 + 1))
        g.dma_start(out=ctx.xb[:, fo, :], in_=ctx.x1_d[fo, :, :]).then_inc(ctx.g_x1b, 16)
    g.wait_ge(ctx.s_cntw, 16)
    g.collective_compute("AllGather", ALU.bypass, replica_groups=[list(range(NC))],
                         ins=[ctx.cnt_l.ap().opt()], outs=[ctx.cnt_g.ap().opt()]).then_inc(ctx.g_cc, 1)
    g.wait_ge(ctx.s_tblz, 33)
    for cl in range(8):
        g.wait_ge(ctx.v_sidx, cl + 1)
        g.indirect_dma_start(
            out=ctx.tbl_l[:, :], out_offset=IndirectOffsetOnAxis(ctx.iidx[:], axis=0),
            in_=ctx.vals[:], in_offset=None).then_inc(ctx.g_scat, 16)
    g.wait_ge(ctx.g_scat, 16 * 8)
    g.collective_compute("AllReduce", ALU.add, replica_groups=[list(range(NC))],
                         ins=[ctx.tbl_l.ap().opt()], outs=[ctx.tbl_g.ap().opt()]).then_inc(ctx.g_cc, 1)
    g.wait_ge(ctx.s_h2w, 16 * 8)
    g.collective_compute("AllGather", ALU.bypass, replica_groups=[list(range(NC))],
                         ins=[ctx.h2_l.ap().opt()], outs=[ctx.h2_g.ap().opt()]).then_inc(ctx.g_cc, 1)
    if ctx.dbg:
        g.dma_start(out=ctx.dbg_h2[:, :], in_=ctx.h2_l[:, :]).then_inc(ctx.g_pr2, 16)
    if st < 4:
        return
    g.wait_ge(ctx.g_cc, 8)
    for j in range(10):
        g.wait_ge(ctx.v_idx2, j + 1)
        g.indirect_dma_start(
            out=ctx.tinfo[:, j, :], out_offset=None,
            in_=ctx.tbl_g[:, :], in_offset=IndirectOffsetOnAxis(ctx.iidx[:], axis=0)).then_inc(ctx.g_tinfo, 16)
    g.wait_ge(ctx.s_pr3, 48)
    g.dma_start(out=ctx.p_bc[:],
                in_=ctx.prow_d.rearrange("j p -> (j p)")[None, :].broadcast_to((128, CAP))
                ).then_inc(ctx.g_pbc, 16)
    g.wait_ge(ctx.g_cc, 9)
    hsrc = ctx.h2_g.rearrange("r t n -> (r t) n")
    g.wait_ge(ctx.v_gidx, 10)
    for j in range(10):
        if j >= 2:
            g.wait_ge(ctx.t_gtb, 2 * (j - 1))
        g.indirect_dma_start(
            out=ctx.gst[:, j % 2, :], out_offset=None,
            in_=hsrc, in_offset=IndirectOffsetOnAxis(ctx.gidx[:, j:j + 1], axis=0)).then_inc(ctx.g_Ggat, 16)
    for io in range(NF):
        if io >= 2:
            g.wait_ge(ctx.t_w1, 3 * (io - 1))
        g.dma_start(out=ctx.w1s[:, io % 2, :, :],
                    in_=ctx.w1_in.rearrange("(a p) m -> p a m", p=128)[:, :, 128 * io:128 * (io + 1)]
                    ).then_inc(ctx.g_w1 if io % 2 == 0 else ctx.g_w1b, 16)
    for fo in range(NF):
        if fo >= 2:
            g.wait_ge(ctx.t_w2, 3 * (fo - 1))
        g.dma_start(out=ctx.w2s[:, fo % 2, :, :],
                    in_=ctx.w2_in.rearrange("(a p) m -> p a m", p=128)[:, :, 128 * fo:128 * (fo + 1)]
                    ).then_inc(ctx.g_w2 if fo % 2 == 0 else ctx.g_w2b, 16)
    g.wait_ge(ctx.s_how, 16 * 10)
    if ctx.dbg:
        g.dma_start(out=ctx.dbg_ho[:, :], in_=ctx.ho_l[:, :]).then_inc(ctx.g_pr2, 16)
    g.collective_compute("AllGather", ALU.bypass, replica_groups=[list(range(NC))],
                         ins=[ctx.ho_l.ap().opt()], outs=[ctx.ho_g.ap().opt()]).then_inc(ctx.g_cc, 1)
    g.wait_ge(ctx.g_cc, 10)
    hosrc = ctx.ho_g.rearrange("r t n -> (r t) n")
    g.wait_ge(ctx.v_moeidx, 8)
    for cl in range(8):
        if cl >= 2:
            g.wait_ge(ctx.t_mtb, 2 * (cl - 1))
        g.indirect_dma_start(
            out=ctx.stg[:, cl % 2, :], out_offset=None,
            in_=hosrc, in_offset=IndirectOffsetOnAxis(ctx.gidx[:, cl:cl + 1], axis=0)).then_inc(ctx.g_moe, 16)


def emit_tensor(ctx, t):
    st = ctx.stage
    psS, psC, psD, psT = ctx.psS, ctx.psC, ctx.psD, ctx.psT
    t.wait_ge(ctx.s_one, 1)
    for fi in range(NF):
        t.wait_ge(ctx.g_wm, 16 * (fi + 1))
        for nt in range(3):
            ins = t.matmul(psS[nt][0:1, :], ctx.silucb[:, fi:fi + 1],
                           ctx.wmodf[:, 512 * nt:512 * (nt + 1)],
                           start=(fi == 0), stop=(fi == NF - 1))
        ins.then_inc(ctx.t_modmm, 1)
    t.wait_ge(ctx.s_modx, 48)
    t.wait_ge(ctx.v_one, 1)
    t.transpose(ctx.psS[0][0:128, 0:96], ctx.modt[:, :], ctx.idf[0:96, 0:96]).then_inc(ctx.t_modmm, 1)
    t.wait_ge(ctx.s_cst, 16 * 6)
    t.wait_ge(ctx.g_cst, 80)
    for fi in range(NF):
        t.wait_ge(ctx.g_xb, 16 * (fi // 4 + 1))
        t.wait_ge(ctx.s_xsq, fi + 1)
        for half in range(2):
            sl = slice(512 * half, 512 * (half + 1))
            t.matmul(psD[0:1, sl], ctx.oneb[:], ctx.xb[:, fi, sl],
                     start=(fi == 0), stop=(fi == NF - 1))
            ins = t.matmul(psC[0:1, sl], ctx.oneb[:], ctx.xsq[:, fi % 2, sl],
                           start=(fi == 0), stop=(fi == NF - 1))
        ins.then_inc(ctx.t_statc, 1)
    def _qtm_dk(fc):
        t.wait_ge(ctx.v_qev, 2 * fc + 2)
        if fc >= 1:
            t.wait_ge(ctx.v_tev, fc)
        for tt in range(8):
            ins = t.transpose(psT[:, 128 * tt:128 * (tt + 1)],
                              ctx.qh_fm[:, fc, 128 * tt:128 * (tt + 1)], ctx.idb[:])
        ins.then_inc(ctx.t_tb, 1)
        if fc == 0:
            t.wait_ge(ctx.v_one, 4)
        t.wait_ge(ctx.s_qsq, fc + 1)
        for half in range(2):
            sl = slice(512 * half, 512 * (half + 1))
            ins = t.matmul(psD[0:1, sl], ctx.oneb[:], ctx.xsq[:, fc % 2, sl],
                           start=True, stop=True)
        ins.then_inc(ctx.t_dk, 1)

    t.wait_ge(ctx.s_h1, NF)
    t.wait_ge(ctx.v_one, 2)
    t.wait_ge(ctx.g_wq, 16)
    for fo in range(NF):
        if fo == 8:
            t.wait_ge(ctx.g_wm1, 16)
        wsrc = ctx.wq0 if fo < 8 else ctx.wq1
        for half in range(2):
            gi = 2 * fo + half
            if gi >= 3:
                t.wait_ge(ctx.v_qev, gi - 2)
            for fi in range(NF):
                ins = t.matmul(psS[gi % 3][:, :],
                               wsrc[:, fi, 128 * (fo % 8):128 * (fo % 8) + 128],
                               ctx.h1b[:, fi, 512 * half:512 * (half + 1)],
                               start=(fi == 0), stop=(fi == NF - 1))
            ins.then_inc(ctx.t_qh, 1)
        if fo >= 2:
            _qtm_dk(fo - 2)
    _qtm_dk(14)
    _qtm_dk(15)
    if st < 2:
        return
    t.wait_ge(ctx.s_dkld, 16)
    for r in range(NC):
        ins = t.transpose(psC[0:128, 128 * r:128 * (r + 1)],
                          ctx.tmpc[:, 0, 128 * r:128 * (r + 1)], ctx.idf[:])
    ins.then_inc(ctx.t_tb, 1)
    t.wait_ge(ctx.s_one, 3)
    for h in range(H):
        b = h % 2
        for c in range(NKC):
            it = h * NKC + c
            for half in range(2):
                j = 2 * it + half
                if j >= 3:
                    t.wait_ge(ctx.sc_eb, j - 2)
                if c == 0 and half == 0:
                    t.wait_ge(ctx.s_kv if h % 2 == 0 else ctx.s_kv1, 32 * (h // 2 + 1))
                t.matmul(psS[j % 3][:, :],
                         ctx.kbuf[b][:, c // 8, 128 * (c % 8):128 * (c % 8) + 128],
                         ctx.qh_fm[:, h, 512 * half:512 * (half + 1)],
                         start=True, stop=True).then_inc(ctx.t_sc, 1)
            if c >= 1:
                _dav(ctx, t, h, c - 1)
        _dav(ctx, t, h, NKC - 1)
    t.wait_ge(ctx.v_ynorm, H)
    for fo in range(NF):
        t.wait_ge(ctx.g_wo0 if fo < 8 else ctx.g_wo1, 16)
        wsrc = ctx.wo_half[0] if fo < 8 else ctx.wo_half[1]
        for half in range(2):
            gi = 2 * fo + half
            if gi >= 3:
                t.wait_ge(ctx.s_t3, gi - 2)
            for fi in range(NF):
                ins = t.matmul(psS[gi % 3][:, :],
                               wsrc[:, fi, 128 * (fo % 8):128 * (fo % 8) + 128],
                               ctx.y_fm[:, fi, 512 * half:512 * (half + 1)],
                               start=(fi == 0), stop=(fi == NF - 1))
            ins.then_inc(ctx.t_wo, 1)
    if st < 3:
        return
    t.wait_ge(ctx.g_x1b, 16 * NF)
    for fi in range(NF):
        t.wait_ge(ctx.s_xsq, NF + fi + 1)
        for half in range(2):
            sl = slice(512 * half, 512 * (half + 1))
            t.matmul(psD[0:1, sl], ctx.oneb[:], ctx.xb[:, fi, sl],
                     start=(fi == 0), stop=(fi == NF - 1))
            ins = t.matmul(psC[0:1, sl], ctx.oneb[:], ctx.xsq[:, fi % 2, sl],
                           start=(fi == 0), stop=(fi == NF - 1))
        ins.then_inc(ctx.t_statc, 1)
    t.wait_ge(ctx.v_one, 7)
    for ch in range(NF):
        t.wait_ge(ctx.s_h2f, ch + 1)
        for half in range(2):
            ins = t.matmul(psD[0:8, 512 * half:512 * (half + 1)],
                           ctx.gw_t[:, ch, :],
                           ctx.tmpc[:, ch % 2, 512 * half:512 * (half + 1)],
                           start=(ch == 0), stop=(ch == NF - 1))
        ins.then_inc(ctx.t_gate, 1)
    t.wait_ge(ctx.s_lg, 1)
    for cl in range(8):
        ins = t.transpose(psC[0:128, 8 * cl:8 * cl + 8], ctx.lgfm[0:8, 128 * cl:128 * (cl + 1)], ctx.idf[0:8, 0:8])
    ins.then_inc(ctx.t_one, 1)
    t.wait_ge(ctx.v_h2b, NF)
    t.wait_ge(ctx.v_tev, 16)
    for tt in range(8):
        for bb in range(2):
            if 2 * tt + bb >= 1:
                t.wait_ge(ctx.v_h2tev, 2 * tt + bb)
            for fo in range(8 * bb, 8 * bb + 8):
                ins = t.transpose(psT[:, 128 * (fo % 8):128 * (fo % 8) + 128],
                                  ctx.xb[:, fo, 128 * tt:128 * (tt + 1)], ctx.idb[:])
            ins.then_inc(ctx.t_h2b, 1)
    t.wait_ge(ctx.v_rt, 8)
    t.wait_ge(ctx.s_lg, 2)
    for cl in range(8):
        ins = t.matmul(psD[0:8, 0:8], ctx.colob[:, cl, :], ctx.ohb[:, cl, :],
                       start=(cl == 0), stop=(cl == 7))
    ins.then_inc(ctx.t_one, 1)
    for cl in range(8):
        ins = t.matmul(psC[0:128, 8 * cl:8 * cl + 8], ctx.trib[:], ctx.ohb[:, cl, :],
                       start=True, stop=True)
    ins.then_inc(ctx.t_one, 1)
    t.wait_ge(ctx.v_one, 8)
    t.wait_ge(ctx.s_cev, 1)
    t.matmul(psD[0:8, 8:16], ctx.trimyb[:, :], ctx.cntb[:, :], start=True, stop=True).then_inc(ctx.t_one, 1)
    if st < 4:
        return
    t.wait_ge(ctx.v_h2tev, 16)
    for j in range(10):
        t.wait_ge(ctx.g_Ggat, 32 * (j // 2 + 1))
        for bb in range(2):
            if 2 * j + bb >= 1:
                t.wait_ge(ctx.v_gtev, 2 * j + bb)
            for fi in range(8 * bb, 8 * bb + 8):
                ins = t.transpose(psT[:, 128 * (fi % 8):128 * (fi % 8) + 128],
                                  ctx.gst[:, j % 2, 128 * fi:128 * (fi + 1)], ctx.idb[:])
            ins.then_inc(ctx.t_gtb, 1)
    t.wait_ge(ctx.v_gtev, 20)
    for io in range(NF):
        t.wait_ge(ctx.g_w1 if io % 2 == 0 else ctx.g_w1b, 16 * (io // 2 + 1))
        for st3 in range(3):
            gi = 3 * io + st3
            if gi >= 3:
                t.wait_ge(ctx.sc_hm, gi - 2)
            n0 = 512 * st3
            n1 = min(CAP, n0 + 512)
            for fi in range(NF):
                ins = t.matmul(psS[gi % 3][:, 0:n1 - n0],
                               ctx.w1s[:, io % 2, fi, :],
                               ctx.G_fm[:, fi, n0:n1],
                               start=(fi == 0), stop=(fi == NF - 1))
            ins.then_inc(ctx.t_w1, 1)
    t.wait_ge(ctx.sc_hm, 48)
    for fo in range(NF):
        t.wait_ge(ctx.g_w2 if fo % 2 == 0 else ctx.g_w2b, 16 * (fo // 2 + 1))
        for st3 in range(3):
            gi = 3 * fo + st3
            if gi >= 3:
                t.wait_ge(ctx.v_ho, gi - 2)
            n0 = 512 * st3
            n1 = min(CAP, n0 + 512)
            for io in range(NF):
                hm = ctx.hmid_lo[:, io, n0:n1] if n1 <= 1024 else ctx.hmid_hi[:, io, n0 - 1024:n1 - 1024]
                ins = t.matmul(psS[gi % 3][:, 0:n1 - n0],
                               ctx.w2s[:, fo % 2, io, :], hm,
                               start=(io == 0), stop=(io == NF - 1))
            ins.then_inc(ctx.t_w2, 1)
    t.wait_ge(ctx.v_ho, 48)
    for j in range(10):
        for bb in range(2):
            if 2 * j + bb >= 1:
                t.wait_ge(ctx.v_htev, 2 * j + bb)
            for fo in range(8 * bb, 8 * bb + 8):
                ins = t.transpose(psT[:, 128 * (fo % 8):128 * (fo % 8) + 128],
                                  ctx.hout_fm[:, fo, 128 * j:128 * (j + 1)], ctx.idb[:])
            ins.then_inc(ctx.t_htb, 1)
    t.wait_ge(ctx.v_htev, 20)
    for cl in range(8):
        t.wait_ge(ctx.v_keep, cl + 1)
        for bb in range(2):
            if 2 * cl + bb >= 1:
                t.wait_ge(ctx.v_mtev, 8 * (2 * cl + bb))
            for fo in range(8 * bb, 8 * bb + 8):
                ins = t.transpose(psT[:, 128 * (fo % 8):128 * (fo % 8) + 128],
                                  ctx.stg[:, cl % 2, 128 * fo:128 * (fo + 1)], ctx.idb[:])
            ins.then_inc(ctx.t_mtb, 1)


def _dav(ctx, t, h, cp):
    it = h * NKC + cp
    t.wait_ge(ctx.sc_eb, 2 * it + 2)
    if cp == 0:
        if h == 0:
            t.wait_ge(ctx.v_dkc, 1)
        t.wait_ge(ctx.v_free, h)
        t.wait_ge(ctx.v_fd, h)
        t.wait_ge(ctx.v_vt, h + 1)
    b = h % 2
    ebv = ctx.eb[:, cp % 2, :]
    st_ = (cp == 0)
    sp = (cp == NKC - 1)
    t.matmul(ctx.psD[0:1, 0:512], ctx.expmb[:, cp // 8, h, cp % 8:cp % 8 + 1], ebv[:, 0:512], start=st_, stop=sp)
    t.matmul(ctx.psD[0:1, 512:1024], ctx.expmb[:, cp // 8, h, cp % 8:cp % 8 + 1], ebv[:, 512:1024], start=st_, stop=sp)
    t.matmul(ctx.psC[:, 0:512], ctx.vbuf[b][:, cp, :], ebv[:, 0:512], start=st_, stop=sp)
    t.matmul(ctx.psC[:, 512:1024], ctx.vbuf[b][:, cp, :], ebv[:, 512:1024], start=st_, stop=sp).then_inc(ctx.t_dav, 1)


def emit_scalar(ctx, s):
    st = ctx.stage
    psS, psC, psD = ctx.psS, ctx.psC, ctx.psD
    s.wait_ge(ctx.s_cst, 16 * 6)
    s.activation(ctx.silucb[:], ctx.c_t[:], AF.Silu).then_inc(ctx.s_one, 1)
    for fi in range(NF):
        s.wait_ge(ctx.g_xb, 16 * (fi // 4 + 1))
        if fi >= 2:
            s.wait_ge(ctx.t_statc, fi - 1)
        s.activation(ctx.xsq[:, fi % 2, :], ctx.xb[:, fi, :], AF.Square).then_inc(ctx.s_xsq, 1)
    s.wait_ge(ctx.v_one, 3)
    s.activation(ctx.tmpc[0:1, 1, 0:1024], ctx.rows[0:1, 0:1024], AF.Sqrt).then_inc(ctx.s_one, 1)
    s.wait_ge(ctx.s_pr1, 32)
    for h in range(H):
        s.wait_ge(ctx.v_qev, 2 * h + 2)
        if h >= 2:
            s.wait_ge(ctx.t_dk, h - 1)
        s.activation(ctx.xsq[:, h % 2, :], ctx.qh_fm[:, h, :], AF.Square).then_inc(ctx.s_qsq, 1)
        s.wait_ge(ctx.t_dk, h + 1)
        if h >= 1:
            s.wait_ge(ctx.s_dkw, 16 * h)
        s.copy(ctx.rows[0:1, 1024:2048], psD[0:1, :]).then_inc(ctx.s_dkev, 1)
    if st < 2:
        return
    s.wait_ge(ctx.v_one, 5)
    s.activation(ctx.expmf[:], ctx.mnegb[:], AF.Exp, scale=-1.0)
    s.drain()
    s.copy(ctx.expmb[:], ctx.expmf[:]).then_inc(ctx.s_one, 1)
    for h in range(H):
        for c in range(NKC):
            it = h * NKC + c
            for half in range(2):
                j = 2 * it + half
                s.wait_ge(ctx.t_sc, j + 1)
                if it >= 2 and half == 0:
                    s.wait_ge(ctx.t_dav, it - 1)
                s.activation(ctx.eb[:, it % 2, 512 * half:512 * (half + 1)],
                             psS[j % 3][:, :], AF.Exp,
                             bias=ctx.mnegb[:, c // 8, h, c % 8:c % 8 + 1], scale=SC).then_inc(ctx.sc_eb, 1)
    if st < 3:
        return
    s.wait_ge(ctx.v_one, 6)
    s.activation(ctx.tmpc[0:1, 1, 0:1024], ctx.rows[0:1, 0:1024], AF.Sqrt).then_inc(ctx.s_one, 1)
    s.wait_ge(ctx.t_gate, NF)
    s.copy(ctx.lgfm[:], psD[0:8, :]).then_inc(ctx.s_lg, 1)
    s.wait_ge(ctx.t_one, 1)
    s.copy(ctx.lgT[:].rearrange("p c e -> p (c e)"), psC[0:128, 0:64]).then_inc(ctx.s_lg, 1)
    for cl in range(8):
        s.wait_ge(ctx.v_rt, cl + 1)
        s.activation(ctx.el8[:, cl, :], ctx.lgT[:, cl, :], AF.Exp,
                     bias=ctx.negm8[:, cl:cl + 1],
                     accum_out=ctx.esum[:, cl:cl + 1]).then_inc(ctx.s_lg, 1)
    s.wait_ge(ctx.t_one, 2)
    s.copy(ctx.cnt_sb[:], psD[0:8, 0:8]).then_inc(ctx.s_cev, 1)
    s.wait_ge(ctx.t_one, 3)
    s.copy(ctx.incl[:].rearrange("p c e -> p (c e)"), psC[0:128, 0:64]).then_inc(ctx.s_cev, 1)
    s.wait_ge(ctx.t_one, 4)
    s.copy(ctx.prefmy[:], psD[0:8, 8:16]).then_inc(ctx.s_cev, 1)
    if st < 4:
        return
    for io in range(NF):
        for st3 in range(3):
            gi = 3 * io + st3
            s.wait_ge(ctx.t_w1, gi + 1)
            n0 = 512 * st3
            n1 = min(CAP, n0 + 512)
            hm = ctx.hmid_lo[:, io, n0:n1] if n1 <= 1024 else ctx.hmid_hi[:, io, n0 - 1024:n1 - 1024]
            s.activation(hm, psS[gi % 3][:, 0:n1 - n0], AF.Gelu).then_inc(ctx.sc_hm, 1)


def emit_vector(ctx, v):
    st = ctx.stage
    psS, psC, psD, psT = ctx.psS, ctx.psC, ctx.psD, ctx.psT
    v.wait_ge(ctx.t_modmm, NF)
    for nt in range(3):
        ins = v.tensor_copy(ctx.rows[0:1, 512 * nt:512 * (nt + 1)], psS[nt][0:1, :])
    ins.then_inc(ctx.v_one, 1)
    v.wait_ge(ctx.t_modmm, NF + 1)
    v.wait_ge(ctx.s_cst, 16 * 6)
    v.tensor_copy(ctx.modc[:].rearrange("p a b -> p (a b)"), ctx.psS[0][0:128, 0:96])
    v.drain()
    for (gi_, dst) in ((2, ctx.gacl), (5, ctx.gbcl)):
        a = ctx.tmpc[:, 0, 0:NF]
        b = ctx.tmpc[:, 0, NF:2 * NF]
        m = ctx.tmpc[:, 0, 2 * NF:3 * NF]
        v.tensor_scalar(a, ctx.modc[:, gi_, :], 0.25, None, ALU.max)
        v.tensor_scalar(b, ctx.modc[:, gi_, :], -0.25, None, ALU.min)
        v.tensor_scalar(m, ctx.modc[:, gi_, :], 0.0, None, ALU.is_ge)
        v.drain()
        v.tensor_sub(dst[:], a, b)
        v.drain()
        v.tensor_mul(dst[:], dst[:], m)
        v.drain()
        v.tensor_add(dst[:], dst[:], b)
        v.drain()
    v.tensor_scalar(ctx.onepca[:], ctx.modc[:, 1, :], 1.0, None, ALU.add)
    v.tensor_scalar(ctx.onepcb[:], ctx.modc[:, 4, :], 1.0, None, ALU.add)
    v.drain()
    v.tensor_mul(ctx.gabo[:], ctx.gacl[:], ctx.bo_t[:]).then_inc(ctx.v_one, 1)
    v.wait_ge(ctx.t_statc, 16)
    _ln_rows(ctx, v, 3, 2)
    v.wait_ge(ctx.s_rows1, 16)
    for ch in range(NF):
        v.tensor_tensor(ctx.tmpc[:, ch % 2, :], ctx.xb[:, ch, :], ctx.rows_bc[:, 0, :], ALU.mult)
        v.drain()
        v.tensor_tensor(ctx.tmpc[:, ch % 2, :], ctx.tmpc[:, ch % 2, :], ctx.rows_bc[:, 1, :], ALU.add)
        v.drain()
        v.tensor_scalar(ctx.h1b[:, ch, :], ctx.tmpc[:, ch % 2, :],
                        ctx.onepca[:, ch:ch + 1], ctx.modc[:, 0, ch:ch + 1],
                        ALU.mult, ALU.add).then_inc(ctx.s_h1, 1)
    def _qtm_evac(fc):
        if fc >= 2:
            v.wait_ge(ctx.g_qtm0 if fc % 2 == 0 else ctx.g_qtm1, 16 * (fc // 2))
        v.wait_ge(ctx.t_tb, fc + 1)
        v.tensor_copy(ctx.tmpc[:, fc % 2, :], psT[:, 0:1024]).then_inc(ctx.v_tev, 1)

    for gi in range(32):
        v.wait_ge(ctx.t_qh, gi + 1)
        fo, half = gi // 2, gi % 2
        v.tensor_copy(ctx.qh_fm[:, fo, 512 * half:512 * (half + 1)], psS[gi % 3][:, :]).then_inc(ctx.v_qev, 1)
        if half == 1 and fo >= 2:
            _qtm_evac(fo - 2)
    _qtm_evac(14)
    _qtm_evac(15)
    if st < 2:
        return
    v.wait_ge(ctx.t_tb, 17)
    v.tensor_copy(ctx.mnegb[:].rearrange("p r h c -> p (r h c)"), psC[0:128, :]).then_inc(ctx.v_dkc, 1)
    v.drain()
    v.tensor_scalar(ctx.mnegb[:], ctx.mnegb[:], -SC, None, ALU.mult).then_inc(ctx.v_one, 1)
    def _vhead_scale(h):
        b = h % 2
        v.wait_ge(ctx.s_kv if h % 2 == 0 else ctx.s_kv1, 32 * (h // 2 + 1))
        v.wait_ge(ctx.s_one, 3)
        if h >= 2:
            v.wait_ge(ctx.t_dav, NKC * (h - 1))
        for c in range(NKC):
            ins = v.tensor_scalar(ctx.vbuf[b][:, c, :], ctx.vbuf[b][:, c, :],
                                  ctx.expmf[:, c // 8, h, c % 8:c % 8 + 1], None, ALU.mult)
        ins.then_inc(ctx.v_vt, 1)

    def _vhead_end(h):
        v.wait_ge(ctx.t_dav, NKC * (h + 1))
        v.tensor_copy(ctx.y_fm[:, h, :], psC[:, :]).then_inc(ctx.v_free, 1)
        v.tensor_copy(ctx.rows[0:1, 0:1024], psD[0:1, :]).then_inc(ctx.v_fd, 1)
        v.drain()
        v.reciprocal(ctx.rows[0:1, 0:1024], ctx.rows[0:1, 0:1024]).then_inc(ctx.v_rcp, 1)
        v.wait_ge(ctx.s_dbc, 16 * (h + 1))
        v.drain()
        v.tensor_tensor(ctx.y_fm[:, h, :], ctx.y_fm[:, h, :], ctx.rows_bc[:, 0, :], ALU.mult).then_inc(ctx.v_ynorm, 1)

    for h in range(H):
        _vhead_scale(h)
        if h >= 1:
            _vhead_end(h - 1)
    _vhead_end(H - 1)
    v.wait_ge(ctx.g_qtm1, 128)
    for fo in range(NF):
        for half in range(2):
            gi = 2 * fo + half
            v.wait_ge(ctx.t_wo, gi + 1)
            v.wait_ge(ctx.s_xs2 if fo % 2 == 0 else ctx.s_xs1, 16 * (fo // 2 + 1))
            v.drain()
            v.tensor_scalar(ctx.t3[:, half, :], psS[gi % 3][:, :],
                            ctx.gacl[:, fo:fo + 1], ctx.gabo[:, fo:fo + 1],
                            ALU.mult, ALU.add).then_inc(ctx.s_t3, 1)
            v.drain()
            v.tensor_tensor(ctx.xs[:, fo % 2, 512 * half:512 * (half + 1)],
                            ctx.xs[:, fo % 2, 512 * half:512 * (half + 1)],
                            ctx.t3[:, half, :], ALU.add).then_inc(ctx.v_x1h, 1)
    if st < 3:
        return
    v.wait_ge(ctx.g_x1b, 16 * NF)
    for fi in range(NF):
        if fi >= 2:
            v.wait_ge(ctx.t_statc, NF + fi - 1)
        v.drain()
        v.tensor_tensor(ctx.xsq[:, fi % 2, :], ctx.xb[:, fi, :], ctx.xb[:, fi, :], ALU.mult).then_inc(ctx.s_xsq, 1)
    v.wait_ge(ctx.t_statc, 32)
    _ln_rows(ctx, v, 6, 4)
    v.wait_ge(ctx.s_rows2, 16)
    for ch in range(NF):
        if ch >= 2:
            v.wait_ge(ctx.t_gate, ch - 1)
        v.drain()
        v.tensor_tensor(ctx.tmpc[:, ch % 2, :], ctx.xb[:, ch, :], ctx.rows_bc[:, 0, :], ALU.mult)
        v.drain()
        v.tensor_tensor(ctx.tmpc[:, ch % 2, :], ctx.tmpc[:, ch % 2, :], ctx.rows_bc[:, 1, :], ALU.add)
        v.drain()
        v.tensor_scalar(ctx.tmpc[:, ch % 2, :], ctx.tmpc[:, ch % 2, :],
                        ctx.onepcb[:, ch:ch + 1], ctx.modc[:, 3, ch:ch + 1],
                        ALU.mult, ALU.add).then_inc(ctx.s_h2f, 1)
        v.drain()
        v.tensor_copy(ctx.xb[:, ch, :], ctx.tmpc[:, ch % 2, :]).then_inc(ctx.v_h2b, 1)
    for tt in range(8):
        if tt >= 1:
            v.wait_ge(ctx.s_h2w, 16 * tt)
        for bb in range(2):
            v.wait_ge(ctx.t_h2b, 2 * tt + bb + 1)
            v.tensor_copy(ctx.stg[:, tt % 2, 1024 * bb:1024 * (bb + 1)], psT[:, 0:1024]).then_inc(ctx.v_h2tev, 1)
    v.wait_ge(ctx.s_lg, 2)
    v.wait_ge(ctx.v_pidx, 2)
    for cl in range(8):
        _routing_chunk(ctx, v, cl)
    v.wait_ge(ctx.s_cntw, 32)
    v.tensor_copy(ctx.cntb[:], ctx.cntf[:]).then_inc(ctx.v_one, 1)
    v.wait_ge(ctx.s_cev, 2)
    v.wait_ge(ctx.s_prefw, 32)
    for cl in range(8):
        _grank_chunk(ctx, v, cl)
    if st < 4:
        return
    v.wait_ge(ctx.g_scat, 16 * 8)
    for j in range(10):
        if j >= 1:
            v.wait_ge(ctx.g_tinfo, 16 * j)
        v.drain()
        v.scalar_tensor_tensor(ctx.sidx_f[:, 0:1], ctx.cid_t[:], float(CAP), ctx.pidx[:], ALU.mult, ALU.add)
        v.drain()
        v.tensor_scalar(ctx.sidx_f[:, 0:1], ctx.sidx_f[:, 0:1], float(128 * j), None, ALU.add)
        v.drain()
        v.tensor_copy(ctx.iidx[:], ctx.sidx_f[:, 0:1]).then_inc(ctx.v_idx2, 1)
    v.wait_ge(ctx.g_tinfo, 160)
    v.drain()
    v.tensor_copy(ctx.pcol[:], ctx.tinfo[:, :, 1]).then_inc(ctx.v_one, 1)
    v.drain()
    v.tensor_copy(ctx.gidx[:], ctx.tinfo[:, :, 0]).then_inc(ctx.v_gidx, 10)
    for j in range(10):
        for bb in range(2):
            v.wait_ge(ctx.t_gtb, 2 * j + bb + 1)
            dst = ctx.G_fm[:, 8 * bb:8 * bb + 8, 128 * j:128 * (j + 1)]
            v.tensor_copy(dst, psT[:, 0:1024].rearrange("p (a n) -> p a n", a=8)).then_inc(ctx.v_gtev, 1)
    v.wait_ge(ctx.g_pbc, 16)
    for fo in range(NF):
        for st3 in range(3):
            gi = 3 * fo + st3
            v.wait_ge(ctx.t_w2, gi + 1)
            n0 = 512 * st3
            n1 = min(CAP, n0 + 512)
            v.tensor_tensor(ctx.hout_fm[:, fo, n0:n1], psS[gi % 3][:, 0:n1 - n0],
                            ctx.p_bc[:, n0:n1], ALU.mult).then_inc(ctx.v_ho, 1)
    for j in range(10):
        if j >= 1:
            v.wait_ge(ctx.s_how, 16 * j)
        for bb in range(2):
            v.wait_ge(ctx.t_htb, 2 * j + bb + 1)
            v.tensor_copy(ctx.stg[:, j % 2, 1024 * bb:1024 * (bb + 1)], psT[:, 0:1024]).then_inc(ctx.v_htev, 1)
    v.drain()
    v.scalar_tensor_tensor(ctx.sidx_f[:, 0:8], ctx.etok[:, :], float(CAP),
                           ctx.grank[:, :], ALU.mult, ALU.add)
    v.drain()
    v.tensor_mul(ctx.sidx_f[:, 0:8], ctx.sidx_f[:, 0:8], ctx.keep[:, :])
    v.drain()
    v.tensor_copy(ctx.gidx[:, 0:8], ctx.sidx_f[:, 0:8]).then_inc(ctx.v_moeidx, 8)
    for cl in range(8):
        v.wait_ge(ctx.g_moe, 32 * (cl // 2 + 1))
        v.drain()
        v.tensor_scalar(ctx.stg[:, cl % 2, :], ctx.stg[:, cl % 2, :],
                        ctx.keep[:, cl:cl + 1], None, ALU.mult).then_inc(ctx.v_keep, 1)
        for bb in range(2):
            v.wait_ge(ctx.t_mtb, 2 * cl + bb + 1)
            dst = ctx.xb[:, 8 * bb:8 * bb + 8, 128 * cl:128 * (cl + 1)]
            v.tensor_copy(dst, psT[:, 0:1024].rearrange("p (a n) -> p a n", a=8)).then_inc(ctx.v_mtev, 8)
    v.wait_ge(ctx.v_mtev, 128)
    for fo in range(NF):
        v.wait_ge(ctx.s_xs2 if fo % 2 == 0 else ctx.s_xs1, 16 * (NF // 2 + fo // 2 + 1))
        v.drain()
        v.scalar_tensor_tensor(ctx.xs[:, fo % 2, :], ctx.xb[:, fo, :],
                               ctx.gbcl[:, fo:fo + 1], ctx.xs[:, fo % 2, :],
                               ALU.mult, ALU.add).then_inc(ctx.v_out, 1)


def _ln_rows(ctx, v, v_val, s_val):
    r0 = ctx.rows[0:1, 0:1024]
    r1 = ctx.rows[0:1, 1024:2048]
    t0 = ctx.tmpc[0:1, 0, 0:1024]
    t1 = ctx.tmpc[0:1, 1, 0:1024]
    v.tensor_scalar(t0, ctx.psD[0:1, :], 1.0 / D, None, ALU.mult)
    v.tensor_scalar(t1, ctx.psC[0:1, :], 1.0 / D, None, ALU.mult)
    v.drain()
    v.tensor_tensor(r0, t0, t0, ALU.mult)
    v.drain()
    v.tensor_sub(r0, t1, r0)
    v.drain()
    v.tensor_scalar(r0, r0, LN_EPS, None, ALU.add).then_inc(ctx.v_one, 1)
    v.wait_ge(ctx.s_one, s_val)
    v.reciprocal(r1, t1)
    v.drain()
    v.tensor_tensor(t1, r1, r1, ALU.mult)
    v.drain()
    v.tensor_mul(t1, t1, r0)
    v.drain()
    v.tensor_scalar(t1, t1, -0.5, 1.5, ALU.mult, ALU.add)
    v.drain()
    v.tensor_mul(r1, r1, t1)
    v.drain()
    v.tensor_copy(r0, r1)
    v.drain()
    v.tensor_mul(r1, r1, t0)
    v.drain()
    v.tensor_scalar(r1, r1, -1.0, None, ALU.mult).then_inc(ctx.v_one, 1)


def _routing_chunk(ctx, v, cl):
    lg = ctx.lgT[:, cl, :]
    v.reduce_max(ctx.m8[:, cl:cl + 1], lg, AXX)
    v.drain()
    v.tensor_scalar(ctx.negm8[:, cl:cl + 1], ctx.m8[:, cl:cl + 1], -1.0, None, ALU.mult)
    v.drain()
    v.tensor_scalar(ctx.ohf[:, cl, :], lg, ctx.m8[:, cl:cl + 1], None, ALU.is_equal)
    v.drain()
    v.tensor_scalar(ctx.el8[:, cl, :], ctx.ohf[:, cl, :], -99.0, 99.0, ALU.mult, ALU.add)
    v.drain()
    v.tensor_add(ctx.el8[:, cl, :], ctx.el8[:, cl, :], ctx.iota8[:])
    v.drain()
    v.tensor_reduce(ctx.etok[:, cl:cl + 1], ctx.el8[:, cl, :], AXX, ALU.min)
    v.drain()
    v.tensor_scalar(ctx.ohf[:, cl, :], ctx.iota8[:], ctx.etok[:, cl:cl + 1], None, ALU.is_equal)
    v.drain()
    v.tensor_copy(ctx.ohb[:, cl, :], ctx.ohf[:, cl, :]).then_inc(ctx.v_rt, 1)


def _grank_chunk(ctx, v, cl):
    v.wait_ge(ctx.s_lg, 3 + cl)
    if cl >= 1:
        v.wait_ge(ctx.g_scat, 16 * cl)
    v.drain()
    v.tensor_tensor(ctx.el8[:, cl, :], ctx.el8[:, cl, :], ctx.ohf[:, cl, :], ALU.mult)
    v.drain()
    v.tensor_reduce(ctx.psel[:, cl:cl + 1], ctx.el8[:, cl, :], AXX, ALU.add)
    v.drain()
    v.reciprocal(ctx.esum[:, cl:cl + 1], ctx.esum[:, cl:cl + 1])
    v.drain()
    v.tensor_mul(ctx.psel[:, cl:cl + 1], ctx.psel[:, cl:cl + 1], ctx.esum[:, cl:cl + 1])
    v.drain()
    v.tensor_scalar(ctx.el8[:, cl, :], ctx.incl[:, cl, :], -1.0, None, ALU.add)
    v.drain()
    v.tensor_add(ctx.el8[:, cl, :], ctx.el8[:, cl, :], ctx.prefbc[:, cl, :])
    v.drain()
    v.tensor_mul(ctx.el8[:, cl, :], ctx.el8[:, cl, :], ctx.ohf[:, cl, :])
    v.drain()
    v.tensor_reduce(ctx.grank[:, cl:cl + 1], ctx.el8[:, cl, :], AXX, ALU.add)
    v.drain()
    v.tensor_scalar(ctx.keep[:, cl:cl + 1], ctx.grank[:, cl:cl + 1], float(CAP), None, ALU.is_lt)
    v.drain()
    v.scalar_tensor_tensor(ctx.sidx_f[:, cl:cl + 1], ctx.etok[:, cl:cl + 1], float(CAP),
                           ctx.grank[:, cl:cl + 1], ALU.mult, ALU.add)
    v.drain()
    v.tensor_scalar(ctx.sidx_f[:, cl:cl + 1], ctx.sidx_f[:, cl:cl + 1], float(-DUMP), None, ALU.add)
    v.drain()
    v.tensor_mul(ctx.sidx_f[:, cl:cl + 1], ctx.sidx_f[:, cl:cl + 1], ctx.keep[:, cl:cl + 1])
    v.drain()
    v.tensor_scalar(ctx.sidx_f[:, cl:cl + 1], ctx.sidx_f[:, cl:cl + 1], float(DUMP), None, ALU.add)
    v.drain()
    v.tensor_copy(ctx.iidx[:], ctx.sidx_f[:, cl:cl + 1])
    v.drain()
    v.tensor_copy(ctx.vals[:, 0:1], ctx.tokid_t[:, cl:cl + 1])
    v.drain()
    v.tensor_copy(ctx.vals[:, 1:2], ctx.psel[:, cl:cl + 1]).then_inc(ctx.v_sidx, 1)


def _prep_inputs(L0_attn_x, L0_attn_Wq, L0_attn_Wo, L0_attn_bo, L0_ffn_gate_weight,
                 L0_ffn_w1, L0_ffn_w2, L0_c, L0_wmod):
    bf = ml_dtypes.bfloat16
    x = np.asarray(L0_attn_x, np.float32).reshape(T, D)
    wq = np.ascontiguousarray(np.asarray(L0_attn_Wq, np.float32))
    wo = np.ascontiguousarray(np.asarray(L0_attn_Wo, np.float32))
    bo = np.asarray(L0_attn_bo, np.float32)
    gw = np.asarray(L0_ffn_gate_weight, np.float32)
    w1 = np.asarray(L0_ffn_w1, np.float32)
    w2 = np.asarray(L0_ffn_w2, np.float32)
    c = np.asarray(L0_c, np.float32).reshape(D)
    wm = np.asarray(L0_wmod, np.float32)

    tri128 = np.triu(np.ones((128, 128), np.float32))
    colones = np.zeros((128, 8, 8), np.float32)
    for cc in range(8):
        colones[:, cc, cc] = 1.0
    identb = np.eye(128, dtype=np.float32)
    identf = np.eye(128, dtype=np.float32)
    onecol = np.ones((128, 1), np.float32)

    in_maps = []
    for i in range(NC):
        sl = slice(TC * i, TC * (i + 1))
        x_i = np.ascontiguousarray(x[sl].T).reshape(NF, 128, TC)
        wm_i = np.ascontiguousarray(wm[:, MODSL * i:MODSL * (i + 1)]).reshape(NF, 128, MODSL)
        tri_my = np.zeros((64, E), np.float32)
        for clc in range(E):
            tri_my[0:8 * i + clc, clc] = 1.0
        tokids = np.ascontiguousarray(
            (TC * i + np.arange(TC, dtype=np.float32)).reshape(8, 128).T)
        in_maps.append({
            "x_in": x_i,
            "wq_in": wq, "wo_in": wo,
            "w1_in": np.ascontiguousarray(w1[i]),
            "w2_in": np.ascontiguousarray(w2[i]),
            "wm_in": wm_i,
            "c_in": np.ascontiguousarray(c.reshape(NF, 128).T),
            "bo_in": np.ascontiguousarray(bo.reshape(NF, 128).T),
            "gw_in": np.ascontiguousarray(gw.T.reshape(NF, 128, E)),
            "tri_my": tri_my,
            "tokids": tokids,
            "cidcol": np.full((128, 1), float(i), np.float32),
            "tri128": tri128, "colones": colones, "identb": identb,
            "identf": identf, "onecol": onecol,
        })
    return in_maps


def kernel(**inputs):
    in_maps = _prep_inputs(**inputs)
    nc, _ = build_program()
    res = run_bass_kernel_spmd(nc, in_maps, core_ids=list(range(NC)))
    outs = []
    for r in res.results:
        outs.append(r["out_d"].reshape(D, TC).T)
    return np.concatenate(outs, axis=0).reshape(1, T, D).astype(np.float32)
